# version 09
# baseline (speedup 1.0000x reference)
import contextlib
import numpy as np
import ml_dtypes
import concourse.bass as bass
import concourse.mybir as mybir
from concourse.bass_utils import run_bass_kernel_spmd

F32 = mybir.dt.float32
BF16 = mybir.dt.bfloat16
I32 = mybir.dt.int32
ALU = mybir.AluOpType
ACT = mybir.ActivationFunctionType
AXX = mybir.AxisListType.X

S, D, H, HD, DL, HID = 2048, 1024, 16, 64, 256, 3072
NSEQ, NCORE, KB = 2, 8, 1024


def _finish(nc, P, blk_holder):
    blk_ctx = nc.Block()
    blk = blk_ctx.__enter__()

    @blk.sync
    def _(e):
        P.replay("sync", e, nc)

    @blk.gpsimd
    def _(e):
        P.replay("gpsimd", e, nc)

    @blk.vector
    def _(e):
        P.replay("vector", e, nc)

    @blk.scalar
    def _(e):
        P.replay("scalar", e, nc)

    @blk.tensor
    def _(e):
        P.replay("tensor", e, nc)

    blk_ctx.__exit__(None, None, None)
    return nc


class Ctr:
    def __init__(self, handle):
        self.h = handle
        self.n = 0


class Plan:
    ENGINES = ("sync", "gpsimd", "vector", "scalar", "tensor")

    def __init__(self):
        self.ops = {e: [] for e in self.ENGINES}
        self.scope = None

    def do(self, eng, fn, wait=(), inc=()):
        targets = [(c.h, v) for c, v in wait if v > 0]
        incs = []
        for c, k in inc:
            c.n += k
            incs.append((c.h, k))
        self.ops[eng].append((targets, fn, incs, self.scope))

    def replay(self, eng_name, handle, nc=None):
        for targets, fn, incs, scope in self.ops[eng_name]:
            ctx = (nc.named_scope(scope) if (scope and nc is not None)
                   else contextlib.nullcontext())
            with ctx:
                for h, v in targets:
                    handle.wait_ge(h, v)
                ins = fn(handle) if fn is not None else None
                for h, k in incs:
                    ins = ins.then_inc(h, k)


_LIM = {"seqs": 2, "pairs": 8, "heads": 2, "chunks": 4, "qmm": True, "mlim": 8, "qevac": True, "qnorm": True, "qn_stage": 3}


def build_program(upto="p3b", dbg_scratch=False):
    nc = bass.Bass("TRN2", target_bir_lowering=False, debug=False,
                   detect_race_conditions=False)

    x_in = nc.dram_tensor("x_in", [NSEQ, S, D], F32, kind="ExternalInput").ap()
    w_q = nc.dram_tensor("w_q", [D, D], F32, kind="ExternalInput").ap()
    w_k = nc.dram_tensor("w_k", [D, D], F32, kind="ExternalInput").ap()
    w_v = nc.dram_tensor("w_v", [D, D], F32, kind="ExternalInput").ap()
    w_o = nc.dram_tensor("w_o", [D, D], F32, kind="ExternalInput").ap()
    w_dn = nc.dram_tensor("w_dn", [D, DL], F32, kind="ExternalInput").ap()
    w_uk = nc.dram_tensor("w_uk", [DL, D], F32, kind="ExternalInput").ap()
    w_uv = nc.dram_tensor("w_uv", [DL, D], F32, kind="ExternalInput").ap()
    wg_i = nc.dram_tensor("wg_i", [HID, D], I32, kind="ExternalInput").ap()
    wu_i = nc.dram_tensor("wu_i", [HID, D], I32, kind="ExternalInput").ap()
    wd_i = nc.dram_tensor("wd_i", [D, HID], I32, kind="ExternalInput").ap()
    y_out = nc.dram_tensor("y_out", [NSEQ, S, D], F32, kind="ExternalOutput").ap()

    kd = "ExternalOutput" if dbg_scratch else "Internal"
    hs_hi = nc.dram_tensor("hs_hi", [NSEQ, H, HD, S], BF16, kind=kd).ap()
    hs_lo = nc.dram_tensor("hs_lo", [NSEQ, H, HD, S], BF16, kind=kd).ap()
    aw_d = nc.dram_tensor("aw_d", [128, 45056], BF16, kind=kd).ap()
    wo_hi_d = nc.dram_tensor("wo_hi_d", [128, 8192], BF16, kind=kd).ap()
    wo_lo_d = nc.dram_tensor("wo_lo_d", [128, 8192], BF16, kind=kd).ap()
    wgt_d = nc.dram_tensor("wgt_d", [D, HID], BF16, kind=kd).ap()
    wut_d = nc.dram_tensor("wut_d", [D, HID], BF16, kind=kd).ap()
    wdt_d = nc.dram_tensor("wdt_d", [HID, D], BF16, kind=kd).ap()
    y1_d = nc.dram_tensor("y1_d", [D, NSEQ * S], F32, kind=kd).ap()
    rden_d = nc.dram_tensor("rden_d", [1, S], F32, kind=kd).ap()
    bias_d = nc.dram_tensor("bias_d", [2, 1], F32, kind=kd).ap()
    if dbg_scratch:
        dbg_q = nc.dram_tensor("dbg_q", [128, H * S], BF16, kind=kd).ap()
        dbg_lh = nc.dram_tensor("dbg_lh", [128, 2 * S], BF16, kind=kd).ap()
        dbg_ll = nc.dram_tensor("dbg_ll", [128, 2 * S], BF16, kind=kd).ap()
        dbg_k = nc.dram_tensor("dbg_k", [128, 2 * S], BF16, kind=kd).ap()
        dbg_vh = nc.dram_tensor("dbg_vh", [128, 16 * 130], BF16, kind=kd).ap()
        dbg_vl = nc.dram_tensor("dbg_vl", [128, 16 * 130], BF16, kind=kd).ap()
        dbg_bias = nc.dram_tensor("dbg_bias", [128, 2], F32, kind=kd).ap()
        dbg_eh = nc.dram_tensor("dbg_eh", [128, S], BF16, kind=kd).ap()
        dbg_el = nc.dram_tensor("dbg_el", [128, S], BF16, kind=kd).ap()
        dbg_sd = nc.dram_tensor("dbg_sd", [128, S], F32, kind=kd).ap()
        dbg_dec = nc.dram_tensor("dbg_dec", [128, 2 * S], F32, kind=kd).ap()
        dbg_cw = nc.dram_tensor("dbg_cw", [128, 16], F32, kind=kd).ap()
        dbg_lam = nc.dram_tensor("dbg_lam", [128, 1], F32, kind=kd).ap()
        dbg_y1in = nc.dram_tensor("dbg_y1in", [128, 4096], F32, kind=kd).ap()
        dbg_y1h = nc.dram_tensor("dbg_y1h", [128, 4096], BF16, kind=kd).ap()
        dbg_y1l = nc.dram_tensor("dbg_y1l", [128, 4096], BF16, kind=kd).ap()
        dbg_hph = nc.dram_tensor("dbg_hph", [128, 12288], BF16, kind=kd).ap()
        dbg_hpl = nc.dram_tensor("dbg_hpl", [128, 12288], BF16, kind=kd).ap()
        dbg_zb = nc.dram_tensor("dbg_zb", [128, 4096], F32, kind=kd).ap()
        dbg_outf = nc.dram_tensor("dbg_outf", [128, 4096], F32, kind=kd).ap()
        dbg_di = nc.dram_tensor("dbg_di", [128, 2 * S], F32, kind=kd).ap()

    AW = {"wq_hi": 0, "wq_lo": 8192, "wkv_hi": 16384, "wkv_lo": 24576,
          "wdn_hi": 32768, "wdn_lo": 34816, "wuk_hi": 36864, "wuk_lo": 38912,
          "wuv_hi": 40960, "wuv_lo": 43008}

    eye_f32 = nc.inline_tensor(np.eye(128, dtype=np.float32), name="c_eye32").ap()
    eye_bf = nc.inline_tensor(np.eye(128).astype(ml_dtypes.bfloat16), name="c_eyebf").ap()
    hmask_c = nc.inline_tensor(
        np.stack([np.r_[np.ones(64), np.zeros(64)],
                  np.r_[np.zeros(64), np.ones(64)]], axis=1).astype(np.float32),
        name="c_hmask").ap()
    ones2_c = nc.inline_tensor(np.ones((128, 2), np.float32), name="c_ones2").ap()
    sel_c = nc.inline_tensor(
        np.stack([np.r_[np.ones(128), np.zeros(128)],
                  np.r_[np.zeros(128), np.ones(128)]]).astype(np.float32),
        name="c_sel").ap()
    o164_c = nc.inline_tensor(np.ones((1, 64), np.float32), name="c_o164").ap()
    o1128_c = nc.inline_tensor(np.ones((1, 128), np.float32), name="c_o1128").ap()
    init_c = nc.inline_tensor(
        np.tile(np.array([[100.0, 127.0, 100000.0, 0.0]], np.float32), (128, 1)),
        name="c_init").ap()
    _f = np.arange(2 * S, dtype=np.float64)[None, :]
    _p = np.arange(128, dtype=np.float64)[:, None]
    absdist_c = nc.inline_tensor(
        np.abs(_f - _p - S).astype(np.float32), name="c_absdist").ap()
    zero2_c = nc.inline_tensor(np.zeros((2, 8), np.float32), name="c_zero2").ap()
    vkh_pat = np.zeros((128, 16 * 130), np.float32)
    vkh_pat[:, 64::130] = 1.0
    vkh_pat[:, 129::130] = 1.0
    vkhi_c = nc.inline_tensor(vkh_pat.astype(ml_dtypes.bfloat16), name="c_vkhi").ap()
    vklo_c = nc.inline_tensor(np.zeros((128, 16 * 130)).astype(ml_dtypes.bfloat16),
                              name="c_vklo").ap()
    zero21_c = nc.inline_tensor(np.zeros((2, 1), np.float32), name="c_zero21").ap()

    P = Plan()
    blk_holder = []

    def sbuf(name, shape, dtype):
        return nc.alloc_sbuf_tensor(name, list(shape), dtype)

    eye32 = sbuf("eye32", [128, 128], F32)
    eyebf = sbuf("eyebf", [128, 128], BF16)
    hmask = sbuf("hmask", [128, 2], F32)
    ones2 = sbuf("ones2", [128, 2], F32)
    sel = sbuf("sel", [2, 256], F32)
    ones164 = sbuf("ones164", [1, 64], F32)
    ones1128 = sbuf("ones1128", [1, 128], F32)
    decay = sbuf("decay", [128, 2 * S], F32)
    lamn = sbuf("lamn", [128, 1], F32)
    sv = sbuf("sv", [128, 1], F32)
    s2v = sbuf("s2v", [128, 1], F32)
    epsv = sbuf("epsv", [128, 1], F32)
    cwork = sbuf("cwork", [128, 16], F32)
    qn2 = sbuf("qn2", [2, 8], F32)
    kn2 = sbuf("kn2", [2, 1], F32)
    biasv = sbuf("biasv", [128, 2], F32)

    AR = 188 * KB
    arena = nc.alloc_sbuf_tensor("arena", [128, AR], mybir.dt.uint8)
    ar_base = nc.lookup_mloc(arena).addr

    def at(name, off_kb, shape, dtype):
        per = int(np.prod(shape[1:])) * mybir.dt.size(dtype)
        off = int(off_kb * KB)
        assert off + per <= AR, (name, off_kb, per)
        return nc.alloc_sbuf_tensor_at(name, list(shape), dtype,
                                       offset=ar_base + off)

    wuk_hi = at("wuk_hi", 0, [128, 2 * KB], BF16)
    wuk_lo = at("wuk_lo", 4, [128, 2 * KB], BF16)
    wuv_hi = at("wuv_hi", 8, [128, 2 * KB], BF16)
    wuv_lo = at("wuv_lo", 12, [128, 2 * KB], BF16)
    qstk = at("qstk", 16, [128, H * S], BF16)
    lat_hi = at("lat_hi", 80, [128, 2 * S], BF16)
    lat_lo = at("lat_lo", 88, [128, 2 * S], BF16)
    stgA = at("stgA", 96, [128, KB], F32)
    stgB = at("stgB", 100, [128, KB], F32)
    stgC = at("stgC", 104, [128, KB], F32)
    shiA = at("shiA", 108, [128, KB], BF16)
    sloA = at("sloA", 110, [128, KB], BF16)
    shiB = at("shiB", 112, [128, KB], BF16)
    sloB = at("sloB", 114, [128, KB], BF16)
    wq_hi = at("wq_hi", 96, [128, 8 * KB], BF16)
    wq_lo = at("wq_lo", 112, [128, 8 * KB], BF16)
    xtokQ = [at("xtokQA", 128, [128, 4 * KB], F32),
             at("xtokQB", 144, [128, 4 * KB], F32)]
    xf_hi = at("xf_hi", 160, [128, 4 * KB], BF16)
    xf_lo = at("xf_lo", 168, [128, 4 * KB], BF16)
    sqb = at("sqb", 176, [128, 512], F32)
    wkv_hi = at("wkv_hi", 96, [128, 8 * KB], BF16)
    wkv_lo = at("wkv_lo", 112, [128, 8 * KB], BF16)
    wdn_hi = at("wdn_hi", 128, [128, 2 * KB], BF16)
    wdn_lo = at("wdn_lo", 132, [128, 2 * KB], BF16)
    xtokS = at("xtokS", 136, [128, 4 * KB], F32)
    xfk_hi = at("xfk_hi", 152, [128, 4 * KB], BF16)
    xfk_lo = at("xfk_lo", 160, [128, 4 * KB], BF16)
    kv_hi = at("kv_hi", 168, [128, 4 * KB], BF16)
    kv_lo = at("kv_lo", 176, [128, 4 * KB], BF16)
    dint = at("dint", 128, [128, 2 * S], I32)
    kstkA0 = at("kstkA0", 96, [128, S], BF16)
    kstkA1 = at("kstkA1", 100, [128, S], BF16)
    kstkB0 = at("kstkB0", 104, [128, S], BF16)
    kstkB1 = at("kstkB1", 108, [128, S], BF16)
    vk_hi = at("vk_hi", 112, [128, 16 * 130], BF16)
    vk_lo = at("vk_lo", 116.25, [128, 16 * 130], BF16)
    sdec = [at("sdec0", 120.5, [128, S], F32), at("sdec1", 128.5, [128, S], F32)]
    Ebuf = [at("Ebuf0", 136.5, [128, S], F32), at("Ebuf1", 144.5, [128, S], F32)]
    ehi3 = [at(f"ehi{i}", 152.5 + 4 * i, [128, S], BF16) for i in range(3)]
    elo3 = [at(f"elo{i}", 164.5 + 4 * i, [128, S], BF16) for i in range(3)]
    hssb_hi = at("hssb_hi", 176.5, [64, S], BF16)
    hssb_lo = at("hssb_lo", 180.5, [64, S], BF16)
    rden = at("rden", 176.5, [1, S], F32)
    rb = at("rb", 176.5, [64, S], F32)
    sqb2 = at("sqb2", 184.5, [128, 512], F32)
    att_hi2 = [at("att_hiA", 0, [128, 4 * KB], BF16),
               at("att_hiB", 118, [128, 4 * KB], BF16)]
    att_lo2 = [at("att_loA", 8, [128, 4 * KB], BF16),
               at("att_loB", 126, [128, 4 * KB], BF16)]
    x3tok = [at("x3tokA", 16, [128, 2 * KB], F32),
             at("x3tokB", 24, [128, 2 * KB], F32)]
    xf3 = at("xf3", 32, [128, 4 * KB], F32)
    r1b = at("r1b", 48, [128, 4 * KB], F32)
    y1b = at("y1b", 64, [128, 4 * KB], F32)
    wo_hi = at("wo_hi", 80, [128, 8 * KB], BF16)
    wo_lo = at("wo_lo", 96, [128, 8 * KB], BF16)
    sq3 = at("sq3", 112, [128, 512], F32)
    mb3 = at("mb3", 114, [128, 512], F32)
    rstdb3 = at("rstdb3", 116, [128, 512], F32)
    wgsb = at("wgsb", 0, [128, 24 * KB], BF16)
    wusb = at("wusb", 48, [128, 24 * KB], BF16)
    hph = at("hph", 96, [128, 12 * KB], BF16)
    hpl = at("hpl", 120, [128, 12 * KB], BF16)
    y1sb = at("y1sb", 144, [128, 4 * KB], F32)
    y1h = at("y1h", 160, [128, 4 * KB], BF16)
    y1l = at("y1l", 168, [128, 4 * KB], BF16)
    wdsb = [at("wdsbA", 176, [128, KB], BF16), at("wdsbB", 178, [128, KB], BF16)]
    silb = at("silb", 180, [128, 512], F32)
    hpb = at("hpb", 182, [128, 512], F32)
    dbuf = [at("dbufA", 184, [128, 512], F32), at("dbufB", 186, [128, 512], F32)]
    outf = at("outf", 96, [128, 4 * KB], F32)
    outtok = at("outtok", 112, [128, 4 * KB], F32)
    zsq = at("zsq", 128, [128, 512], F32)
    mbf = at("mbf", 130, [128, 512], F32)
    rstdbf = at("rstdbf", 132, [128, 512], F32)

    psa = [nc.alloc_psum_tensor(f"psa{i}", [128, 512], F32) for i in range(4)]
    psb = [nc.alloc_psum_tensor(f"psb{i}", [128, 512], F32) for i in range(4)]

    dl = Ctr(nc.alloc_semaphore("s_dl"))
    dg = Ctr(nc.alloc_semaphore("s_dg"))
    ds = Ctr(nc.alloc_semaphore("s_ds"))
    te = Ctr(nc.alloc_semaphore("s_te"))
    ve = Ctr(nc.alloc_semaphore("s_ve"))
    se = Ctr(nc.alloc_semaphore("s_se"))
    gp = Ctr(nc.alloc_semaphore("s_gp"))
    QSEM = {"sync": dl, "gpsimd": dg, "scalar": ds}
    swd = [Ctr(nc.alloc_semaphore("s_wd0")), Ctr(nc.alloc_semaphore("s_wd1"))]
    swg = [Ctr(nc.alloc_semaphore("s_wg0")), Ctr(nc.alloc_semaphore("s_wg1"))]
    satt = [Ctr(nc.alloc_semaphore("s_at0")), Ctr(nc.alloc_semaphore("s_at1"))]
    srd = Ctr(nc.alloc_semaphore("s_rd"))
    sbias = Ctr(nc.alloc_semaphore("s_bias"))

    def dma(q, dst, src, wait=(), ctr=None):
        c = ctr if ctr is not None else QSEM[q]
        P.do(q, lambda e, d=dst, s=src: e.dma_start(out=d, in_=s),
             wait=wait, inc=[(c, 16)])
        return (c, c.n)

    def split_to(v, hi_ap, lo_ap, src_ap):
        v.tensor_copy(hi_ap, src_ap)
        return v.tensor_sub(lo_ap, src_ap, hi_ap)

    P.scope = "p0"
    c_eye = dma("sync", eye32[:, :], eye_f32)
    dma("sync", eyebf[:, :], eye_bf)
    dma("sync", hmask[:, :], hmask_c)
    dma("sync", ones2[:, :], ones2_c)
    dma("sync", sel[:, :], sel_c)
    dma("sync", ones164[:, :], o164_c)
    const_ld = dma("sync", ones1128[:, :], o1128_c)
    init_ld = dma("sync", cwork[:, 0:4], init_c)

    def emit_recips(v):
        v.reciprocal(cwork[:, 4:5], cwork[:, 0:1])
        v.reciprocal(sv[:, :], cwork[:, 1:2])
        return v.reciprocal(epsv[:, :], cwork[:, 2:3])
    P.do("vector", emit_recips, wait=[init_ld], inc=[(ve, 1)])
    recips_done = (ve, ve.n)

    def emit_consts(v):
        v.tensor_scalar_mul(lamn[:, :], cwork[:, 4:5], -1.0)
        return v.tensor_mul(s2v[:, :], sv[:, :], sv[:, :])
    P.do("vector", emit_consts, wait=[recips_done], inc=[(ve, 1)])
    consts_done = (ve, ve.n)

    absd_ld = dma("sync", dint[:, :].bitcast(F32), absdist_c)
    if dbg_scratch:
        absd_ld = dma("scalar", dbg_di, dint[:, :].bitcast(F32), wait=[absd_ld])
    P.do("scalar", lambda s_: s_.activation(decay[:, :], dint[:, :].bitcast(F32),
                                            ACT.Exp, bias=0.0, scale=lamn[:, :]),
         wait=[absd_ld, consts_done], inc=[(se, 1)])
    decay_done = (se, se.n)

    def ffn_transpose(src_i32, n_r, n_c, row_w, dst_d):
        casts = []
        first_w = [(te, te.n)]
        for r in range(n_r):
            casts.append(dma("gpsimd", wgsb[:, r * row_w:(r + 1) * row_w],
                             src_i32[r * 128:(r + 1) * 128, :],
                             wait=(first_w if r == 0 else [])))
        all_cast = casts[-1]
        strip_st = None
        for c in range(n_c):
            ev_tail = None
            for r4 in range(0, n_r, 4):
                nn = min(4, n_r - r4)
                bank = (r4 // 4) % 2
                for i in range(nn):
                    r = r4 + i
                    if r4 == 0 and i == 0:
                        w0 = [all_cast, const_ld, (ve, ve.n)]
                    elif i == 0:
                        w0 = [(ve, ve.n)]
                    else:
                        w0 = []
                    P.do("tensor", lambda t, r=r, c=c, bank=bank, i=i, w=row_w:
                         t.transpose(psb[bank][:, i * 64:(i + 1) * 64].bitcast(BF16),
                                     wgsb[:, r * w + c * 128:r * w + (c + 1) * 128],
                                     eyebf[:, :]),
                         wait=w0, inc=([(te, 1)] if i == nn - 1 else []))
                tp_done = (te, te.n)
                ws = [tp_done] + ([strip_st] if (r4 == 0 and strip_st) else [])
                P.do("vector", lambda v, r4=r4, nn=nn, bank=bank:
                     v.tensor_copy(wusb[:, r4 * 128:(r4 + nn) * 128],
                                   psb[bank][:, 0:nn * 64].bitcast(BF16)),
                     wait=ws, inc=[(ve, 1)])
                ev_tail = (ve, ve.n)
            strip_st = dma("scalar", dst_d[c * 128:(c + 1) * 128, :],
                           wusb[:, 0:n_r * 128], wait=[ev_tail])
        return strip_st

    ffn_transpose(wg_i, HID // 128, D // 128, D, wgt_d)
    ffn_transpose(wu_i, HID // 128, D // 128, D, wut_d)
    ffn_transpose(wd_i, D // 128, HID // 128, HID, wdt_d)
    ffnw_done = (ds, ds.n)

    prev_split = [None, None]
    prev_store = [None, None]

    def w_chunk(par, srcs, hi_dst, lo_dst, cols):
        stg = stgA if par == 0 else stgB
        shi = shiA if par == 0 else shiB
        slo = sloA if par == 0 else sloB
        w0 = [prev_split[par]] if prev_split[par] else []
        dma("sync", stg[:, 0:cols], srcs[0], wait=w0)
        if len(srcs) > 1:
            dma("sync", stgC[:, 0:cols], srcs[1])
        ld_t = (dl, dl.n)

        def fn(v, stg=stg, shi=shi, slo=slo, cols=cols, two=len(srcs) > 1):
            if two:
                v.tensor_add(stg[:, 0:cols], stg[:, 0:cols], stgC[:, 0:cols])
            return split_to(v, shi[:, 0:cols], slo[:, 0:cols], stg[:, 0:cols])
        ws = [ld_t] + ([prev_store[par]] if prev_store[par] else [])
        P.do("vector", fn, wait=ws, inc=[(ve, 1)])
        prev_split[par] = (ve, ve.n)
        dma("sync", hi_dst, shi[:, 0:cols], wait=[prev_split[par]])
        prev_store[par] = dma("sync", lo_dst, slo[:, 0:cols],
                              wait=[prev_split[par]])

    for r in range(8):
        w_chunk(r % 2, [w_q[r * 128:(r + 1) * 128, :]],
                aw_d[:, AW["wq_hi"] + r * D:AW["wq_hi"] + (r + 1) * D],
                aw_d[:, AW["wq_lo"] + r * D:AW["wq_lo"] + (r + 1) * D], D)
    for r in range(8):
        w_chunk(r % 2, [w_k[r * 128:(r + 1) * 128, :], w_v[r * 128:(r + 1) * 128, :]],
                aw_d[:, AW["wkv_hi"] + r * D:AW["wkv_hi"] + (r + 1) * D],
                aw_d[:, AW["wkv_lo"] + r * D:AW["wkv_lo"] + (r + 1) * D], D)
    for r in range(8):
        w_chunk(r % 2, [w_dn[r * 128:(r + 1) * 128, :]],
                aw_d[:, AW["wdn_hi"] + r * DL:AW["wdn_hi"] + (r + 1) * DL],
                aw_d[:, AW["wdn_lo"] + r * DL:AW["wdn_lo"] + (r + 1) * DL], DL)
    for r in range(2):
        w_chunk(r % 2, [w_uk[r * 128:(r + 1) * 128, :]],
                aw_d[:, AW["wuk_hi"] + r * D:AW["wuk_hi"] + (r + 1) * D],
                aw_d[:, AW["wuk_lo"] + r * D:AW["wuk_lo"] + (r + 1) * D], D)
    for r in range(2):
        w_chunk(r % 2, [w_uv[r * 128:(r + 1) * 128, :]],
                aw_d[:, AW["wuv_hi"] + r * D:AW["wuv_hi"] + (r + 1) * D],
                aw_d[:, AW["wuv_lo"] + r * D:AW["wuv_lo"] + (r + 1) * D], D)
    for r in range(8):
        w_chunk(r % 2, [w_o[r * 128:(r + 1) * 128, :]],
                wo_hi_d[:, r * D:(r + 1) * D], wo_lo_d[:, r * D:(r + 1) * D], D)
    aw_done_ve = (ve, ve.n)
    aw_store_done = (dl, dl.n)

    dma("sync", wuk_hi[:, :], aw_d[:, AW["wuk_hi"]:AW["wuk_hi"] + 2 * KB],
        wait=[aw_store_done])
    dma("sync", wuk_lo[:, :], aw_d[:, AW["wuk_lo"]:AW["wuk_lo"] + 2 * KB])
    dma("sync", wuv_hi[:, :], aw_d[:, AW["wuv_hi"]:AW["wuv_hi"] + 2 * KB])
    wu_res = dma("sync", wuv_lo[:, :], aw_d[:, AW["wuv_lo"]:AW["wuv_lo"] + 2 * KB])

    if upto == "p0":
        P.do("sync", None, wait=[wu_res])
        return _finish(nc, P, blk_holder)
    TCH = 512
    NCH = S // TCH
    hs_st_tail = None
    p2_tail_ve, p2_tail_te, p2_tail_se = None, None, None

    for s in range(_LIM["seqs"]):
        P.scope = "p1q_s%d" % s
        gate_w = [(te, te.n), (ve, ve.n), aw_store_done]
        dma("sync", wq_hi[:, :], aw_d[:, AW["wq_hi"]:AW["wq_hi"] + 8 * KB],
            wait=gate_w)
        wq_ld = dma("sync", wq_lo[:, :], aw_d[:, AW["wq_lo"]:AW["wq_lo"] + 8 * KB])

        qn_init = dma("sync", qn2[:, :], zero2_c, wait=gate_w)

        tp_tail = {}
        tp_tail[-2] = decay_done
        tp_tail[-1] = decay_done
        q_bank_free = [None, None]

        def qevac_fn(v, m, bank, chv):
            h0, h1 = 2 * m, 2 * m + 1
            c0 = chv * TCH
            p0 = psa[bank][0:64, 0:TCH]
            p1 = psa[bank][64:128, 0:TCH]
            v.tensor_copy(qstk[0:64, h0 * S + c0:h0 * S + c0 + TCH], p0)
            v.tensor_sub(qstk[64:128, h0 * S + c0:h0 * S + c0 + TCH], p0,
                         qstk[0:64, h0 * S + c0:h0 * S + c0 + TCH])
            v.tensor_copy(qstk[0:64, h1 * S + c0:h1 * S + c0 + TCH], p1)
            return v.tensor_sub(qstk[64:128, h1 * S + c0:h1 * S + c0 + TCH],
                                p1, qstk[0:64, h1 * S + c0:h1 * S + c0 + TCH])
        for chv in range(_LIM["chunks"]):
            buf = xtokQ[chv % 2]
            lw = [wq_ld] + ([tp_tail[chv - 2]] if chv - 2 in tp_tail else [])
            xl = dma("sync", buf[:, :].rearrange("p (a d) -> p a d", a=4),
                     x_in[s, chv * TCH:(chv + 1) * TCH, :]
                     .rearrange("(a p) d -> p a d", p=128), wait=lw)
            for f in range(8):
                bank = f % 2
                for tt in range(4):
                    if f == 0 and tt == 0:
                        w0 = [xl, wq_ld, (ve, ve.n)]
                    elif tt == 0:
                        w0 = [(ve, ve.n)]
                    else:
                        w0 = []
                    P.do("tensor", lambda t, f=f, tt=tt, bank=bank, b=buf:
                         t.transpose(psb[bank][:, tt * 128:(tt + 1) * 128],
                                     b[:, tt * D + f * 128:tt * D + (f + 1) * 128],
                                     eye32[:, :]),
                         wait=w0, inc=([(te, 1)] if tt == 3 else []))
                tpd = (te, te.n)
                P.do("vector", lambda v, f=f, bank=bank: split_to(
                    v, xf_hi[:, f * TCH:(f + 1) * TCH],
                    xf_lo[:, f * TCH:(f + 1) * TCH],
                    psb[bank][:, 0:TCH]),
                    wait=[tpd], inc=[(ve, 1)])
            tp_tail[chv] = (te, te.n)
            xf_done = (ve, ve.n)

            if not _LIM["qmm"]:
                continue
            for m in range(_LIM["mlim"]):
                bank = m % 2
                terms = []
                for r in range(8):
                    terms += [(wq_hi, xf_hi, r), (wq_hi, xf_lo, r), (wq_lo, xf_hi, r)]
                for ti, (lh, rh, r) in enumerate(terms):
                    w0 = []
                    if ti == 0:
                        w0 = [xf_done]
                        if q_bank_free[bank]:
                            w0.append(q_bank_free[bank])
                    P.do("tensor", lambda t, m=m, r=r, lh=lh, rh=rh, bank=bank,
                         first=(ti == 0), last=(ti == len(terms) - 1):
                         t.matmul(psa[bank][:, 0:TCH],
                                  lh[:, r * D + m * 128:r * D + (m + 1) * 128],
                                  rh[:, r * TCH:(r + 1) * TCH],
                                  start=first, stop=last),
                         wait=w0, inc=([(te, 1)] if ti == len(terms) - 1 else []))
                q_mm = (te, te.n)
                if not _LIM["qnorm"]:
                    if _LIM["qevac"]:
                        P.do("vector", lambda v, m=m, bank=bank, chv=chv: qevac_fn(
                            v, m, bank, chv), wait=[q_mm], inc=[(ve, 1)])
                        q_bank_free[bank] = (ve, ve.n)
                    else:
                        P.do("vector", lambda v, bank=bank: v.tensor_copy(
                            sqb[:, :], psa[bank][:, 0:TCH]), wait=[q_mm],
                            inc=[(ve, 1)])
                        q_bank_free[bank] = (ve, ve.n)
                    continue
                P.do("vector", lambda v, m=m, bank=bank, chv=chv: qevac_fn(
                    v, m, bank, chv), wait=[q_mm, qn_init], inc=[(ve, 1)])
                qe_t = (ve, ve.n)

                if _LIM["qn_stage"] >= 1:
                    P.do("scalar", lambda sc, bank=bank:
                         sc.activation(sqb[:, :], psa[bank][:, 0:TCH],
                                       ACT.Square), wait=[q_mm, qe_t],
                         inc=[(se, 1)])
                    sq_done = (se, se.n)
                    q_bank_free[bank] = (se, se.n)
                else:
                    q_bank_free[bank] = qe_t

                if _LIM["qn_stage"] >= 2:
                    P.do("tensor", lambda t: t.matmul(
                        psb[2][0:2, 0:TCH], hmask[:, :], sqb[:, :],
                        start=True, stop=True),
                        wait=[sq_done, (ve, ve.n)], inc=[(te, 1)])
                    nrm_mm = (te, te.n)

                if _LIM["qn_stage"] >= 3:
                    def qn_acc(v, m=m):
                        v.reduce_max(cwork[0:2, 5:6], psb[2][0:2, 0:TCH], axis=AXX)
                        return v.tensor_max(qn2[0:2, m:m + 1], qn2[0:2, m:m + 1],
                                            cwork[0:2, 5:6])
                    P.do("vector", qn_acc, wait=[nrm_mm, qn_init], inc=[(ve, 1)])
        passq_te = (te, te.n)
        passq_ve = (ve, ve.n)

        P.scope = "p1kv_s%d" % s
        if upto == "p1q":
            continue
        dma("sync", wkv_hi[:, :], aw_d[:, AW["wkv_hi"]:AW["wkv_hi"] + 8 * KB],
            wait=[passq_te, passq_ve])
        dma("sync", wkv_lo[:, :], aw_d[:, AW["wkv_lo"]:AW["wkv_lo"] + 8 * KB])
        dma("sync", wdn_hi[:, :], aw_d[:, AW["wdn_hi"]:AW["wdn_hi"] + 2 * KB])
        wkv_ld = dma("sync", wdn_lo[:, :], aw_d[:, AW["wdn_lo"]:AW["wdn_lo"] + 2 * KB])

        kv_bank_free = [None, None]
        lat_free = None
        prev_tp = None
        for chv in range(NCH):
            lw = [wkv_ld] + ([prev_tp] if prev_tp else [])
            xl = dma("sync", xtokS[:, :].rearrange("p (a d) -> p a d", a=4),
                     x_in[s, chv * TCH:(chv + 1) * TCH, :]
                     .rearrange("(a p) d -> p a d", p=128), wait=lw)
            for f in range(8):
                bank = f % 2
                for tt in range(4):
                    if f == 0 and tt == 0:
                        w0 = [xl, (ve, ve.n)]
                    elif tt == 0:
                        w0 = [(ve, ve.n)]
                    else:
                        w0 = []
                    P.do("tensor", lambda t, f=f, tt=tt, bank=bank:
                         t.transpose(psb[bank][:, tt * 128:(tt + 1) * 128],
                                     xtokS[:, tt * D + f * 128:tt * D + (f + 1) * 128],
                                     eye32[:, :]),
                         wait=w0, inc=([(te, 1)] if tt == 3 else []))
                tpd = (te, te.n)
                P.do("vector", lambda v, f=f, bank=bank: split_to(
                    v, xfk_hi[:, f * TCH:(f + 1) * TCH],
                    xfk_lo[:, f * TCH:(f + 1) * TCH],
                    psb[bank][:, 0:TCH]),
                    wait=[tpd], inc=[(ve, 1)])
            prev_tp = (te, te.n)
            xf_done = (ve, ve.n)

            for m in range(8):
                bank = m % 2
                terms = []
                for r in range(8):
                    terms += [(wkv_hi, xfk_hi, r), (wkv_hi, xfk_lo, r),
                              (wkv_lo, xfk_hi, r)]
                for ti, (lh, rh, r) in enumerate(terms):
                    w0 = []
                    if ti == 0:
                        w0 = [xf_done]
                        if kv_bank_free[bank]:
                            w0.append(kv_bank_free[bank])
                    P.do("tensor", lambda t, m=m, r=r, lh=lh, rh=rh, bank=bank,
                         first=(ti == 0), last=(ti == len(terms) - 1):
                         t.matmul(psa[bank][:, 0:TCH],
                                  lh[:, r * D + m * 128:r * D + (m + 1) * 128],
                                  rh[:, r * TCH:(r + 1) * TCH],
                                  start=first, stop=last),
                         wait=w0, inc=([(te, 1)] if ti == len(terms) - 1 else []))
                kv_mm = (te, te.n)
                P.do("vector", lambda v, m=m, bank=bank: split_to(
                    v, kv_hi[:, m * TCH:(m + 1) * TCH],
                    kv_lo[:, m * TCH:(m + 1) * TCH],
                    psa[bank][:, 0:TCH]),
                    wait=[kv_mm], inc=[(ve, 1)])
                kv_bank_free[bank] = (ve, ve.n)
            kv_all = (ve, ve.n)

            for lm in range(2):
                terms = []
                for r in range(8):
                    terms += [(wdn_hi, kv_hi, r), (wdn_hi, kv_lo, r),
                              (wdn_lo, kv_hi, r)]
                for ti, (lh, rh, r) in enumerate(terms):
                    w0 = []
                    if ti == 0:
                        w0 = [kv_all]
                        if lat_free:
                            w0.append(lat_free)
                    P.do("tensor", lambda t, lm=lm, r=r, lh=lh, rh=rh,
                         first=(ti == 0), last=(ti == len(terms) - 1):
                         t.matmul(psb[3][:, 0:TCH],
                                  lh[:, r * DL + lm * 128:r * DL + (lm + 1) * 128],
                                  rh[:, r * TCH:(r + 1) * TCH],
                                  start=first, stop=last),
                         wait=w0, inc=([(te, 1)] if ti == len(terms) - 1 else []))
                lat_mm = (te, te.n)
                P.do("vector", lambda v, lm=lm, chv=chv: split_to(
                    v, lat_hi[:, lm * S + chv * TCH:lm * S + (chv + 1) * TCH],
                    lat_lo[:, lm * S + chv * TCH:lm * S + (chv + 1) * TCH],
                    psb[3][:, 0:TCH]),
                    wait=[lat_mm], inc=[(ve, 1)])
                lat_free = (ve, ve.n)
        p1_ve = (ve, ve.n)
        p1_te = (te, te.n)
        if dbg_scratch and s == 0:
            dma("scalar", dbg_q, qstk[:, :], wait=[p1_ve])
            dma("scalar", dbg_lh, lat_hi[:, :], wait=[p1_ve])
            dma("scalar", dbg_ll, lat_lo[:, :], wait=[p1_ve])
            dma("scalar", dbg_dec, decay[:, :], wait=[p1_ve])
            dma("scalar", dbg_cw, cwork[:, :], wait=[p1_ve])
            dma("scalar", dbg_lam, lamn[:, :], wait=[p1_ve])

        P.scope = "p2_s%d" % s
        if upto in ("p1", "p1q"):
            continue

        dma("sync", vk_hi[:, :], vkhi_c, wait=[p1_ve, wu_res])
        vk_ones_done = dma("sync", vk_lo[:, :], vklo_c)

        xh = {"dk15": (ve, ve.n), "hsplit": (ve, ve.n), "pv15": (te, te.n),
              "exp14": (se, se.n), "exp15": (se, se.n),
              "ehi14": (ve, ve.n), "ehi15": (ve, ve.n),
              "gp14": (gp, gp.n), "gp15": (gp, gp.n),
              "vel14": (ve, ve.n), "vel15": (ve, ve.n)}
        pending_tail = [None]
        for pair in range(_LIM["pairs"]):
            kn_init = dma("sync", kn2[:, :], zero21_c, wait=[(ve, ve.n)])
            kevac_t = {}
            kn_acc_prev = None
            for nb in range(4):
                bank = nb % 2
                terms = []
                for r in range(2):
                    terms += [(wuk_hi, lat_hi, r), (wuk_hi, lat_lo, r),
                              (wuk_lo, lat_hi, r)]
                for ti, (lh, rh, r) in enumerate(terms):
                    w0 = ([p1_ve, xh["dk15"]] +
                          ([kevac_t[nb - 2]] if nb - 2 in kevac_t else [])
                          ) if ti == 0 else []
                    P.do("tensor", lambda t, lh=lh, rh=rh, r=r, nb=nb, bank=bank,
                         pair=pair, first=(ti == 0), last=(ti == len(terms) - 1):
                         t.matmul(psa[bank][:, :],
                                  lh[:, r * D + pair * 128:r * D + (pair + 1) * 128],
                                  rh[:, r * S + nb * 512:r * S + (nb + 1) * 512],
                                  start=first, stop=last),
                         wait=w0, inc=([(te, 1)] if ti == len(terms) - 1 else []))
                k_mm = (te, te.n)

                def kevac(v, nb=nb, bank=bank):
                    c0 = nb * 512
                    p0 = psa[bank][0:64, :]
                    p1 = psa[bank][64:128, :]
                    v.tensor_copy(kstkA0[0:64, c0:c0 + 512], p0)
                    v.tensor_sub(kstkA0[64:128, c0:c0 + 512], p0,
                                 kstkA0[0:64, c0:c0 + 512])
                    v.tensor_copy(kstkA1[0:64, c0:c0 + 512], p1)
                    v.tensor_sub(kstkA1[64:128, c0:c0 + 512], p1,
                                 kstkA1[0:64, c0:c0 + 512])
                    v.tensor_copy(kstkB0[0:64, c0:c0 + 512],
                                  kstkA0[64:128, c0:c0 + 512])
                    v.tensor_copy(kstkB0[64:128, c0:c0 + 512],
                                  kstkA0[0:64, c0:c0 + 512])
                    v.tensor_copy(kstkB1[0:64, c0:c0 + 512],
                                  kstkA1[64:128, c0:c0 + 512])
                    return v.tensor_copy(kstkB1[64:128, c0:c0 + 512],
                                         kstkA1[0:64, c0:c0 + 512])
                P.do("vector", kevac, wait=[k_mm], inc=[(ve, 1)])
                kevac_t[nb] = (ve, ve.n)
                P.do("scalar", lambda sc, bank=bank:
                     sc.activation(sqb2[:, :], psa[bank][:, :],
                                   ACT.Square), wait=[k_mm, kevac_t[nb]],
                     inc=[(se, 1)])
                ksq = (se, se.n)
                kevac_t[nb] = ksq
                P.do("tensor", lambda t: t.matmul(
                    psa[2][0:2, :], hmask[:, :], sqb2[:, :],
                    start=True, stop=True),
                    wait=[ksq, kn_init, xh["dk15"]] +
                         ([kn_acc_prev] if kn_acc_prev else []),
                    inc=[(te, 1)])
                kn_mm = (te, te.n)

                def kn_acc(v):
                    v.reduce_max(cwork[0:2, 6:7], psa[2][0:2, :], axis=AXX)
                    return v.tensor_max(kn2[:, :], kn2[:, :], cwork[0:2, 6:7])
                P.do("vector", kn_acc, wait=[kn_mm, kn_init], inc=[(ve, 1)])
                kn_acc_prev = (ve, ve.n)
            kn_done = (ve, ve.n)

            P.do("vector", lambda v, pair=pair: v.tensor_mul(
                cwork[0:2, 7:8], qn2[0:2, pair:pair + 1], kn2[:, :]),
                wait=[kn_done, (se, se.n)], inc=[(ve, 1)])
            bp = (ve, ve.n)
            P.do("scalar", lambda sc: sc.activation(
                cwork[0:2, 8:9], cwork[0:2, 7:8], ACT.Sqrt),
                wait=[bp], inc=[(se, 1)])
            bsq = (se, se.n)
            P.do("vector", lambda v: v.tensor_scalar_mul(
                cwork[0:2, 9:10], cwork[0:2, 8:9], -1.05 / 32.0),
                wait=[bsq], inc=[(ve, 1)])
            bneg = (ve, ve.n)
            b_st = dma("gpsimd", bias_d, cwork[0:2, 9:10], wait=[bneg])
            bias_view, _ = bass.broadcast_tensor_aps(
                bias_d.rearrange("a b -> b a"), biasv[:, 0:2])
            bias_done = dma("gpsimd", biasv[:, 0:2], bias_view,
                            wait=[b_st, (se, se.n)], ctr=sbias)

            vevac_t = {}
            for tt in range(16):
                bank = 2 + tt % 2
                terms = []
                for r in range(2):
                    terms += [(lat_hi, wuv_hi, r), (lat_lo, wuv_hi, r),
                              (lat_hi, wuv_lo, r)]
                for ti, (lh, rh, r) in enumerate(terms):
                    w0 = ([p1_ve, xh["dk15"], vk_ones_done, kn_done] +
                          ([vevac_t[tt - 2]] if tt - 2 in vevac_t else [])
                          ) if ti == 0 else []
                    P.do("tensor", lambda t, lh=lh, rh=rh, r=r, tt=tt, bank=bank,
                         pair=pair, first=(ti == 0), last=(ti == len(terms) - 1):
                         t.matmul(psa[bank][:, 0:128],
                                  lh[:, r * S + tt * 128:r * S + (tt + 1) * 128],
                                  rh[:, r * D + pair * 128:r * D + (pair + 1) * 128],
                                  start=first, stop=last),
                         wait=w0, inc=([(te, 1)] if ti == len(terms) - 1 else []))
                v_mm = (te, te.n)

                def vevac(v, tt=tt, bank=bank):
                    c0 = tt * 130
                    pa = psa[bank][:, 0:64]
                    pb = psa[bank][:, 64:128]
                    v.tensor_copy(vk_hi[:, c0:c0 + 64], pa)
                    v.tensor_sub(vk_lo[:, c0:c0 + 64], pa, vk_hi[:, c0:c0 + 64])
                    v.tensor_copy(vk_hi[:, c0 + 65:c0 + 129], pb)
                    return v.tensor_sub(vk_lo[:, c0 + 65:c0 + 129], pb,
                                        vk_hi[:, c0 + 65:c0 + 129])
                P.do("vector", vevac, wait=[v_mm], inc=[(ve, 1)])
                vevac_t[tt] = (ve, ve.n)
            vk_done = (ve, ve.n)
            if dbg_scratch and s == 0 and pair == 0:
                dma("scalar", dbg_k[:, 0:S], kstkA0[:, :], wait=[vk_done])
                dma("scalar", dbg_k[:, S:2 * S], kstkB0[:, :], wait=[vk_done])
                dma("scalar", dbg_vh, vk_hi[:, :], wait=[vk_done])
                dma("scalar", dbg_vl, vk_lo[:, :], wait=[vk_done])
                dma("scalar", dbg_bias, biasv[:, :], wait=[vk_done, bias_done])

            for hh in range(_LIM["heads"]):
                hd = 2 * pair + hh
                kA = kstkA0 if hh == 0 else kstkA1
                kB = kstkB0 if hh == 0 else kstkB1
                te_scores_b, te_pv = {}, {}
                dk_b, se_exp = {}, {}
                ve_ehi, gp_elo, ve_elo = {}, {}, {}

                def emit_scores(jt):
                    for ib in range(4):
                        for pi, kk in ((0, kA), (1, kB)):
                            if pi == 0 and jt > 0:
                                w0 = [dk_b[jt - 1][ib]]
                            elif pi == 0 and ib == 0:
                                w0 = [vk_done, bias_done, xh["dk15"]]
                            else:
                                w0 = []
                            P.do("tensor", lambda t, jt=jt, kk=kk, ib=ib, pi=pi,
                                 hd=hd:
                                 t.matmul(psa[ib][:, :],
                                          kk[:, jt * 128:(jt + 1) * 128],
                                          qstk[:, hd * S + ib * 512:
                                               hd * S + (ib + 1) * 512],
                                          start=(pi == 0), stop=(pi == 1)),
                                 wait=w0,
                                 inc=([(te, 1)] if pi == 1 else []))
                        te_scores_b[(jt, ib)] = (te, te.n)

                def emit_softmax(jt):
                    sd = sdec[jt % 2]
                    Eb = Ebuf[jt % 2]
                    eh, el = ehi3[jt % 3], elo3[jt % 3]
                    dk_b[jt] = []
                    for ib in range(4):
                        w0 = [te_scores_b[(jt, ib)]]
                        if ib == 0:
                            if jt >= 2:
                                w0.append(se_exp[jt - 2])
                            else:
                                w0.append(xh["exp14" if jt == 0 else "exp15"])
                        P.do("vector", lambda v, jt=jt, sd=sd, ib=ib:
                             v.tensor_mul(sd[:, ib * 512:(ib + 1) * 512],
                                          psa[ib][:, :],
                                          decay[:, S - jt * 128 + ib * 512:
                                                S - jt * 128 + (ib + 1) * 512]),
                             wait=w0, inc=[(ve, 1)])
                        dk_b[jt].append((ve, ve.n))
                    w1 = [dk_b[jt][3]]
                    if jt >= 2:
                        w1 += [gp_elo[jt - 2], ve_elo[jt - 2], ve_ehi[jt - 2]]
                    else:
                        sfx = "14" if jt == 0 else "15"
                        w1 += [xh["gp" + sfx], xh["vel" + sfx], xh["ehi" + sfx]]
                    P.do("scalar", lambda sc, sd=sd, Eb=Eb, hh=hh:
                         sc.activation(Eb[:, :], sd[:, :], ACT.Exp,
                                       bias=biasv[:, hh:hh + 1], scale=0.03125),
                         wait=w1, inc=[(se, 1)])
                    se_exp[jt] = (se, se.n)
                    w2 = [se_exp[jt]]
                    w2.append(te_pv[jt - 3] if jt >= 3 else xh["pv15"])
                    P.do("vector", lambda v, Eb=Eb, eh=eh:
                         v.tensor_copy(eh[:, :], Eb[:, :]),
                         wait=w2, inc=[(ve, 1)])
                    ve_ehi[jt] = (ve, ve.n)
                    w3 = [ve_ehi[jt]]
                    w3.append(te_pv[jt - 3] if jt >= 3 else xh["pv15"])
                    P.do("gpsimd", lambda g, Eb=Eb, eh=eh, el=el:
                         g.tensor_sub(el[:, 0:1792], Eb[:, 0:1792],
                                      eh[:, 0:1792]),
                         wait=w3, inc=[(gp, 1)])
                    gp_elo[jt] = (gp, gp.n)
                    P.do("vector", lambda v, Eb=Eb, eh=eh, el=el:
                         v.tensor_sub(el[:, 1792:2048], Eb[:, 1792:2048],
                                      eh[:, 1792:2048]),
                         wait=[], inc=[(ve, 1)])
                    ve_elo[jt] = (ve, ve.n)

                def emit_pv(jt):
                    eh, el = ehi3[jt % 3], elo3[jt % 3]
                    base = jt * 130 + hh * 65
                    w0 = [gp_elo[jt], ve_elo[jt]]
                    if jt == 0:
                        w0 += [xh["hsplit"], vk_done]
                        if hh == 0:
                            w0.append(kn_done)
                    terms = [(vk_hi, eh), (vk_lo, eh), (vk_hi, el)]
                    for ti, (vkk, ee) in enumerate(terms):
                        for ib in range(4):
                            P.do("tensor", lambda t, vkk=vkk, ee=ee, ib=ib,
                                 ti=ti, jt=jt, base=base:
                                 t.matmul(psb[ib][0:65, :],
                                          vkk[:, base:base + 65],
                                          ee[:, ib * 512:(ib + 1) * 512],
                                          start=(jt == 0 and ti == 0),
                                          stop=(jt == 15 and ti == 2)),
                                 wait=(w0 if (ti == 0 and ib == 0) else []),
                                 inc=([(te, 1)] if (ti == 2 and ib == 3) else []))
                    te_pv[jt] = (te, te.n)

                for jt in range(16):
                    emit_scores(jt)
                    emit_softmax(jt)
                    if jt == 1 and pending_tail[0] is not None:
                        xh["hsplit"] = pending_tail[0]()
                        hs_st_tail = pending_tail[0].holder["st"]
                        pending_tail[0] = None
                    if dbg_scratch and s == 0 and pair == 0 and hh == 0 and jt == 0:
                        dma("scalar", dbg_sd, sdec[0][:, :], wait=[dk_b[0][3]])
                        dma("scalar", dbg_eh, ehi3[0][:, :], wait=[ve_ehi[0]])
                        dma("scalar", dbg_el, elo3[0][:, :],
                            wait=[gp_elo[0], ve_elo[0]])
                    if jt - 2 >= 0:
                        emit_pv(jt - 2)
                emit_pv(14)
                emit_pv(15)

                def recip4(v):
                    ins = None
                    for ib in range(4):
                        ins = v.reciprocal(rden[0:1, ib * 512:(ib + 1) * 512],
                                           psb[ib][64:65, :])
                    return ins
                hs_drained = [hs_st_tail] if hs_st_tail else []
                P.do("vector", recip4, wait=[te_pv[15]] + hs_drained,
                     inc=[(ve, 1)])
                recip_done = (ve, ve.n)
                rd_st = dma("gpsimd", rden_d, rden[0:1, :],
                            wait=[recip_done])
                bview, _ = bass.broadcast_tensor_aps(rden_d, rb[:, :])
                rb_ld = dma("gpsimd", rb[:, :], bview, wait=[rd_st], ctr=srd)

                def mk_tail(hd=hd, s=s, rb_ld=rb_ld, hs_prev=hs_st_tail):
                    holder = {}

                    def tail():
                        def houtm(v):
                            ins = None
                            for ib in range(4):
                                ins = v.tensor_mul(
                                    psb[ib][64:128, :], psb[ib][0:64, :],
                                    rb[:, ib * 512:(ib + 1) * 512])
                            return ins
                        P.do("vector", houtm, wait=[rb_ld], inc=[(ve, 1)])

                        def hsplit(v):
                            ins = None
                            for ib in range(4):
                                c0 = ib * 512
                                v.tensor_copy(hssb_hi[:, c0:c0 + 512],
                                              psb[ib][64:128, :])
                                ins = v.tensor_sub(hssb_lo[:, c0:c0 + 512],
                                                   psb[ib][64:128, :],
                                                   hssb_hi[:, c0:c0 + 512])
                            return ins
                        P.do("vector", hsplit,
                             wait=([hs_prev] if hs_prev else []),
                             inc=[(ve, 1)])
                        hspl = (ve, ve.n)
                        dma("gpsimd", hs_hi[s, hd], hssb_hi[:, :], wait=[hspl])
                        holder["st"] = dma("gpsimd", hs_lo[s, hd],
                                           hssb_lo[:, :])
                        return hspl
                    tail.holder = holder
                    return tail
                pending_tail[0] = mk_tail()

                xh = {"dk15": dk_b[15][3], "pv15": te_pv[15],
                      "exp14": se_exp[14], "exp15": se_exp[15],
                      "ehi14": ve_ehi[14], "ehi15": ve_ehi[15],
                      "gp14": gp_elo[14], "gp15": gp_elo[15],
                      "vel14": ve_elo[14], "vel15": ve_elo[15],
                      "hsplit": xh["hsplit"]}
        if pending_tail[0] is not None:
            xh["hsplit"] = pending_tail[0]()
            hs_st_tail = pending_tail[0].holder["st"]
            pending_tail[0] = None
        p2_tail_ve = (ve, ve.n)
        p2_tail_te = (te, te.n)
        p2_tail_se = (se, se.n)
    hs_all = (dg, dg.n)
    if upto in ("p1", "p1q", "p2"):
        P.do("sync", None, wait=[hs_all])
        return _finish(nc, P, blk_holder)

    P.scope = "p3a"
    p2w = [p2_tail_ve, p2_tail_te]
    wo_ld1 = dma("sync", wo_hi[:, :], wo_hi_d, wait=p2w + [aw_store_done])
    wo_ld = dma("sync", wo_lo[:, :], wo_lo_d)
    NT = NSEQ * S // 512

    def ln_block(srcbuf, sqbuf, mbuf, rstdbuf, dstbuf, src_ready_ve,
                 apply_extra=(), psum_free=None):
        rl = (list(src_ready_ve) if isinstance(src_ready_ve, list)
              else [src_ready_ve] * 8)
        src_ready_ve = rl[-1]
        sq_ts = []
        sum_tail = None
        for fc in range(8):
            w_sq = [rl[fc]] + ([sum_tail] if sum_tail else [])
            P.do("scalar", lambda sc, fc=fc: sc.activation(
                sqbuf[:, :], srcbuf[:, fc * 512:(fc + 1) * 512], ACT.Square),
                wait=w_sq, inc=[(se, 1)])
            sq_t = (se, se.n)
            P.do("tensor", lambda t, fc=fc: t.matmul(
                psb[2][0:2, :], ones2[:, :],
                srcbuf[:, fc * 512:(fc + 1) * 512],
                start=(fc == 0), stop=(fc == 7)),
                wait=[rl[fc]] + ([psum_free] if (fc == 0 and psum_free)
                                 else []),
                inc=([(te, 1)] if fc == 7 else []))
            P.do("tensor", lambda t, fc=fc: t.matmul(
                psb[3][0:2, :], ones2[:, :], sqbuf[:, :],
                start=(fc == 0), stop=(fc == 7)),
                wait=[sq_t], inc=[(te, 1)])
            sum_tail = (te, te.n)
        stats_mm = (te, te.n)

        def rows(v):
            v.tensor_scalar_mul(mbuf[0:1, :], psb[2][0:1, :], 1.0 / 1024.0)
            v.tensor_scalar_mul(rstdbuf[0:1, :], psb[3][0:1, :], 1.0 / 1024.0)
            v.tensor_mul(sqbuf[0:1, :], mbuf[0:1, :], mbuf[0:1, :])
            v.tensor_sub(rstdbuf[0:1, :], rstdbuf[0:1, :], sqbuf[0:1, :])
            return v.tensor_scalar_add(rstdbuf[0:1, :], rstdbuf[0:1, :],
                                       epsv[0:1, 0:1])
        P.do("vector", rows, wait=[stats_mm], inc=[(ve, 1)])
        ve_rows = (ve, ve.n)
        s0row = dstbuf[0:1, 512:1024]
        trow = dstbuf[0:1, 0:512]
        P.do("scalar", lambda sc: sc.activation(s0row, rstdbuf[0:1, :],
                                                ACT.Sqrt),
             wait=[ve_rows], inc=[(se, 1)])
        se_sqrt = (se, se.n)

        P.do("vector", lambda v: v.reciprocal(sqbuf[0:1, :], s0row),
             wait=[se_sqrt], inc=[(ve, 1)])
        nrecip_done = (ve, ve.n)

        def newton(v):
            y = sqbuf[0:1, :]
            a = rstdbuf[0:1, :]
            for _ in range(2):
                v.tensor_mul(trow, y, y)
                v.tensor_mul(trow, trow, a)
                v.tensor_scalar(trow, trow, -0.5, 1.5, ALU.mult, ALU.add)
                v.tensor_mul(y, y, trow)
            return v.tensor_copy(rstdbuf[0:1, :], y)
        P.do("vector", newton, wait=[nrecip_done], inc=[(ve, 1)])
        ve_newt = (ve, ve.n)
        P.do("tensor", lambda t: t.matmul(psb[2][:, :], ones1128[:, :],
                                          mbuf[0:1, :], start=True, stop=True),
             wait=[ve_newt], inc=[(te, 1)])
        P.do("tensor", lambda t: t.matmul(psb[3][:, :], ones1128[:, :],
                                          rstdbuf[0:1, :], start=True, stop=True),
             inc=[(te, 1)])
        bc_t = (te, te.n)

        def evac_bc(v):
            v.tensor_copy(mbuf[:, :], psb[2][:, :])
            return v.tensor_copy(rstdbuf[:, :], psb[3][:, :])
        P.do("vector", evac_bc, wait=[bc_t], inc=[(ve, 1)])
        bc_v = (ve, ve.n)
        for fc in range(8):
            def apply(v, fc=fc):
                v.tensor_sub(dstbuf[:, fc * 512:(fc + 1) * 512],
                             srcbuf[:, fc * 512:(fc + 1) * 512], mbuf[:, :])
                return v.tensor_mul(dstbuf[:, fc * 512:(fc + 1) * 512],
                                    dstbuf[:, fc * 512:(fc + 1) * 512],
                                    rstdbuf[:, :])
            P.do("vector", apply,
                 wait=([bc_v] + list(apply_extra)) if fc == 0 else [],
                 inc=([(ve, 1)] if fc == 7 else []))
        return (ve, ve.n), (te, te.n)

    prev_ln_ve = p2_tail_ve
    prev_womm = None
    womm_tail = {}
    tp3_tail = {}
    y1st_prev = None
    for mt in range(NT):
        s = mt // 4
        tok0 = (mt % 4) * 512
        att_hi = att_hi2[mt % 2]
        att_lo = att_lo2[mt % 2]
        att_w = [hs_all] + p2w + ([womm_tail[mt - 2]] if mt >= 2 else [])
        att_ld = None
        for kc in range(8):
            dma("gpsimd", att_hi[:, kc * 512:(kc + 1) * 512],
                hs_hi[s, 2 * kc:2 * kc + 2, :, tok0:tok0 + 512]
                .rearrange("a q t -> (a q) t"),
                wait=(att_w if kc == 0 else []), ctr=satt[mt % 2])
            att_ld = dma("gpsimd", att_lo[:, kc * 512:(kc + 1) * 512],
                         hs_lo[s, 2 * kc:2 * kc + 2, :, tok0:tok0 + 512]
                         .rearrange("a q t -> (a q) t"), ctr=satt[mt % 2])
        xf3_done = None
        for chv in range(2):
            buf = x3tok[chv]
            lw = p2w + ([tp3_tail[(mt - 1, chv)]] if (mt - 1, chv) in tp3_tail else [])
            xl = dma("sync", buf[:, :].rearrange("p (a d) -> p a d", a=2),
                     x_in[s, tok0 + chv * 256:tok0 + (chv + 1) * 256, :]
                     .rearrange("(a p) d -> p a d", p=128), wait=lw)
            for f in range(8):
                bank = f % 2
                for tt in range(2):
                    if f == 0 and tt == 0:
                        w0 = [xl, prev_ln_ve, wo_ld, (ve, ve.n)]
                    elif tt == 0:
                        w0 = [(ve, ve.n)]
                    else:
                        w0 = []
                    P.do("tensor", lambda t, f=f, tt=tt, bank=bank, b=buf:
                         t.transpose(psa[2 + bank][:, tt * 128:(tt + 1) * 128],
                                     b[:, tt * D + f * 128:tt * D + (f + 1) * 128],
                                     eye32[:, :]),
                         wait=w0, inc=([(te, 1)] if tt == 1 else []))
                tpd = (te, te.n)
                P.do("vector", lambda v, f=f, chv=chv, bank=bank:
                     v.tensor_copy(xf3[:, f * 512 + chv * 256:
                                       f * 512 + (chv + 1) * 256],
                                   psa[2 + bank][:, 0:256]),
                     wait=[tpd], inc=[(ve, 1)])
            tp3_tail[(mt, chv)] = (te, te.n)
            xf3_done = (ve, ve.n)

        r1_tails = []
        for fc in range(8):
            bank = fc % 2
            terms = []
            for kc in range(8):
                terms += [(wo_hi, att_hi, kc), (wo_hi, att_lo, kc),
                          (wo_lo, att_hi, kc)]
            for ti, (lh, rh, kc) in enumerate(terms):
                w0 = []
                if ti == 0:
                    w0 = [att_ld, wo_ld, xf3_done]
                    if fc >= 2:
                        w0.append(r1_tails[fc - 2])
                P.do("tensor", lambda t, lh=lh, rh=rh, kc=kc, fc=fc, bank=bank,
                     first=(ti == 0), last=(ti == len(terms) - 1):
                     t.matmul(psa[bank][:, :],
                              lh[:, kc * D + fc * 128:kc * D + (fc + 1) * 128],
                              rh[:, kc * 512:(kc + 1) * 512],
                              start=first, stop=last),
                     wait=w0, inc=([(te, 1)] if ti == len(terms) - 1 else []))
            wo_mm = (te, te.n)
            P.do("vector", lambda v, fc=fc, bank=bank: v.tensor_add(
                r1b[:, fc * 512:(fc + 1) * 512], psa[bank][:, :],
                xf3[:, fc * 512:(fc + 1) * 512]),
                wait=[wo_mm], inc=[(ve, 1)])
            r1_tails.append((ve, ve.n))
        prev_womm = (te, te.n)
        womm_tail[mt] = prev_womm
        r1_done = r1_tails[-1]

        ap_extra = [y1st_prev] if y1st_prev else []
        ln1_ve, ln1_te = ln_block(r1b, sq3, mb3, rstdb3, y1b, r1_tails,
                                  apply_extra=ap_extra, psum_free=prev_ln_ve)
        for fc in range(8):
            y1st_prev = dma(
                "sync", y1_d[fc * 128:(fc + 1) * 128, s * S + tok0:s * S + tok0 + 512],
                y1b[:, fc * 512:(fc + 1) * 512],
                wait=([ln1_ve] if fc == 0 else []))
        prev_ln_ve = ln1_ve
    y1_all = (dl, dl.n)
    p3a_te = (te, te.n)
    p3a_ve = (ve, ve.n)
    if upto == "p3a":
        P.do("sync", None, wait=[y1_all])
        return _finish(nc, P, blk_holder)

    P.scope = "p3b"
    prev_gu_tail = None
    prev_ln2_ve = p3a_ve
    prev_yst = None
    prev_outtp = None
    prev_dn_done = None
    prev_zmul = None
    for mt in range(NT):
        s = mt // 4
        tok0 = (mt % 4) * 512
        wG = [p3a_te, p3a_ve, ffnw_done] if mt == 0 else [gu_tail]
        if dbg_scratch and mt == 4:
            wG = wG + dbg_mt_w
        wg_ld = None
        for c in range(8):
            dma("sync", wgsb[:, c * HID:(c + 1) * HID],
                wgt_d[c * 128:(c + 1) * 128, :], wait=(wG if c == 0 else []),
                ctr=swg[mt % 2])
            wg_ld = dma("sync", wusb[:, c * HID:(c + 1) * HID],
                        wut_d[c * 128:(c + 1) * 128, :], ctr=swg[mt % 2])
        y1_ld = dma("scalar", y1sb[:, :].rearrange("p (f t) -> p f t", f=8),
                    y1_d[:, s * S + tok0:s * S + tok0 + 512]
                    .rearrange("(f p) t -> p f t", p=128),
                    wait=[y1_all, prev_ln2_ve] +
                         (dbg_mt_w if dbg_scratch and mt == 4 else []))

        def y1split(v):
            ins = None
            for fc in range(8):
                v.tensor_copy(y1h[:, fc * 512:(fc + 1) * 512],
                              y1sb[:, fc * 512:(fc + 1) * 512])
                ins = v.tensor_sub(y1l[:, fc * 512:(fc + 1) * 512],
                                   y1sb[:, fc * 512:(fc + 1) * 512],
                                   y1h[:, fc * 512:(fc + 1) * 512])
            return ins
        P.do("vector", y1split, wait=[y1_ld], inc=[(ve, 1)])
        y1sp = (ve, ve.n)
        dbg_mt_w = []
        if dbg_scratch and mt == 3:
            dbg_mt_w.append(dma("scalar", dbg_y1in, y1sb[:, :], wait=[y1sp]))
            dbg_mt_w.append(dma("scalar", dbg_y1h, y1h[:, :], wait=[y1sp]))
            dbg_mt_w.append(dma("scalar", dbg_y1l, y1l[:, :], wait=[y1sp]))

        hp_tails = []
        for m in range(24):
            bank = m % 2
            for which in range(2):
                wsb = wgsb if which == 0 else wusb
                po = 0 if which == 0 else 1024
                terms = []
                for c in range(8):
                    terms += [(c, y1h), (c, y1l)]
                for ti, (c, rh) in enumerate(terms):
                    w0 = []
                    if ti == 0 and which == 0:
                        w0 = [y1sp, wg_ld]
                        if m >= 2:
                            w0.append(hp_tails[m - 2])
                        if m == 0 and prev_zmul:
                            w0.append(prev_zmul)
                    P.do("tensor", lambda t, m=m, c=c, rh=rh, wsb=wsb, po=po,
                         bank=bank, first=(ti == 0), last=(ti == len(terms) - 1):
                         t.matmul((psa[bank] if po == 0 else psa[2 + bank])[:, :],
                                  wsb[:, c * HID + m * 128:c * HID + (m + 1) * 128],
                                  rh[:, c * 512:(c + 1) * 512],
                                  start=first, stop=last),
                         wait=w0, inc=([(te, 1)] if (ti == len(terms) - 1 and
                                                     which == 1) else []))
            gu_mm = (te, te.n)
            P.do("scalar", lambda sc, bank=bank: sc.activation(
                silb[:, :], psa[bank][:, :], ACT.Silu,
                scale=sv[:, 0:1]),
                wait=[gu_mm] + ([hp_tails[m - 1]] if m >= 1 else []),
                inc=[(se, 1)])
            sil_t = (se, se.n)

            def hprime(v, m=m, bank=bank):
                v.tensor_mul(hpb[:, :], silb[:, :], psa[2 + bank][:, :])
                v.tensor_copy(hph[:, m * 512:(m + 1) * 512], hpb[:, :])
                return v.tensor_sub(hpl[:, m * 512:(m + 1) * 512], hpb[:, :],
                                    hph[:, m * 512:(m + 1) * 512])
            P.do("vector", hprime, wait=[sil_t], inc=[(ve, 1)])
            hp_tails.append((ve, ve.n))
        gu_tail = (te, te.n)
        prev_gu_tail = gu_tail
        if dbg_scratch and mt == 3:
            dbg_mt_w.append(dma("scalar", dbg_hph, hph[:, :], wait=[hp_tails[23]]))
            dbg_mt_w.append(dma("scalar", dbg_hpl, hpl[:, :], wait=[hp_tails[23]]))

        dn_m_tail = {}
        wd_lds = {}
        wl0 = [prev_dn_done] if prev_dn_done is not None else []
        wd_lds[0] = dma("sync", wdsb[0][:, :], wdt_d[0:128, :], wait=wl0,
                        ctr=swd[0])
        wd_lds[1] = dma("sync", wdsb[1][:, :], wdt_d[128:256, :], wait=wl0,
                        ctr=swd[1])
        for m in range(24):
            wsb = wdsb[m % 2]
            for fc in range(8):
                pbank = psa[fc] if fc < 4 else psb[fc - 4]
                for ri, rh in enumerate((hph, hpl)):
                    w0 = []
                    if fc == 0 and ri == 0:
                        w0 = [hp_tails[m], wd_lds[m]]
                        if m == 0:
                            w0.append(hp_tails[23])
                    P.do("tensor", lambda t, m=m, fc=fc, rh=rh, wsb=wsb,
                         pbank=pbank, ri=ri:
                         t.matmul(pbank[:, :],
                                  wsb[:, fc * 128:(fc + 1) * 128],
                                  rh[:, m * 512:(m + 1) * 512],
                                  start=(m == 0 and ri == 0),
                                  stop=(m == 23 and ri == 1)),
                         wait=w0,
                         inc=([(te, 1)] if (fc == 7 and ri == 1) else []))
            dn_m_tail[m] = (te, te.n)
            if m + 2 < 24:
                wd_lds[m + 2] = dma("sync", wdsb[(m + 2) % 2][:, :],
                                    wdt_d[(m + 2) * 128:(m + 3) * 128, :],
                                    wait=[dn_m_tail[m]], ctr=swd[m % 2])
        dn_done = (te, te.n)
        prev_dn_done = dn_done

        z_tails = []
        for fc in range(8):
            pbank = psa[fc] if fc < 4 else psb[fc - 4]
            db = dbuf[fc % 2]
            w0 = [dn_done]
            if fc >= 2:
                w0.append(z_tails[fc - 2])
            P.do("scalar", lambda sc, db=db, pbank=pbank: sc.mul(
                db[:, :], pbank[:, :], s2v[:, 0:1]),
                wait=w0, inc=[(se, 1)])
            dmul = (se, se.n)
            if fc == 3:
                prev_zmul = dmul
            if fc == 7:
                zmul_last = dmul
            P.do("vector", lambda v, fc=fc, db=db: v.tensor_add(
                y1sb[:, fc * 512:(fc + 1) * 512], db[:, :],
                y1sb[:, fc * 512:(fc + 1) * 512]),
                wait=[dmul], inc=[(ve, 1)])
            z_tails.append((ve, ve.n))
        z_done = z_tails[-1]
        if dbg_scratch and mt == 3:
            dbg_mt_w.append(dma("scalar", dbg_zb, y1sb[:, :], wait=[z_done]))

        ln2_ve, ln2_te = ln_block(y1sb, zsq, mbf, rstdbf, outf, z_tails,
                                  psum_free=zmul_last)

        ot_ev = None
        for tt in range(4):
            for fg in range(2):
                bank = fg % 2
                for f4 in range(4):
                    fc = fg * 4 + f4
                    if tt == 0 and fg == 0 and f4 == 0:
                        w0 = [ln2_ve]
                    elif f4 == 0:
                        w0 = [(ve, ve.n)]
                    else:
                        w0 = []
                    P.do("tensor", lambda t, tt=tt, fc=fc, f4=f4, bank=bank:
                         t.transpose(psb[2 + bank][:, f4 * 128:(f4 + 1) * 128],
                                     outf[:, fc * 512 + tt * 128:
                                          fc * 512 + (tt + 1) * 128],
                                     eye32[:, :]),
                         wait=w0, inc=([(te, 1)] if f4 == 3 else []))
                tpd = (te, te.n)
                P.do("vector", lambda v, tt=tt, fg=fg, bank=bank:
                     v.tensor_copy(outtok[:, tt * D + fg * 512:
                                          tt * D + (fg + 1) * 512],
                                   psb[2 + bank][:, :]),
                     wait=([tpd, prev_yst] if (tt == 0 and fg == 0 and prev_yst)
                           else [tpd]), inc=[(ve, 1)])
                ot_ev = (ve, ve.n)
        prev_outtp = (te, te.n)
        prev_yst = dma("sync",
                       y_out[s, tok0:tok0 + 512, :].rearrange("(a p) d -> p a d", p=128),
                       outtok[:, :].rearrange("p (a d) -> p a d", a=4),
                       wait=[ot_ev])
        prev_ln2_ve = ln2_ve
        if dbg_scratch and mt == 3:
            dbg_mt_w.append(dma("scalar", dbg_outf, outf[:, :], wait=[ln2_ve]))
            prev_gu_tail = None
        if dbg_scratch and mt == 4:
            pass

    P.do("sync", None, wait=[prev_yst])
    return _finish(nc, P, blk_holder)


_NC_CACHE = None


def kernel(L0_attn_x, L0_attn_W_q, L0_attn_W_k, L0_attn_W_v, L0_attn_W_o,
           L0_attn_W_down_kv, L0_attn_W_up_k, L0_attn_W_up_v,
           L0_ffn_w_gate, L0_ffn_w_up, L0_ffn_w_down):
    global _NC_CACHE
    x = np.ascontiguousarray(np.asarray(L0_attn_x, dtype=np.float32))
    if _NC_CACHE is None:
        _NC_CACHE = build_program()
    nc = _NC_CACHE
    weights = {
        "w_q": np.ascontiguousarray(np.asarray(L0_attn_W_q, np.float32)),
        "w_k": np.ascontiguousarray(np.asarray(L0_attn_W_k, np.float32)),
        "w_v": np.ascontiguousarray(np.asarray(L0_attn_W_v, np.float32)),
        "w_o": np.ascontiguousarray(np.asarray(L0_attn_W_o, np.float32)),
        "w_dn": np.ascontiguousarray(np.asarray(L0_attn_W_down_kv, np.float32)),
        "w_uk": np.ascontiguousarray(np.asarray(L0_attn_W_up_k, np.float32)),
        "w_uv": np.ascontiguousarray(np.asarray(L0_attn_W_up_v, np.float32)),
        "wg_i": np.ascontiguousarray(np.asarray(L0_ffn_w_gate, np.int32)),
        "wu_i": np.ascontiguousarray(np.asarray(L0_ffn_w_up, np.int32)),
        "wd_i": np.ascontiguousarray(np.asarray(L0_ffn_w_down, np.int32)),
    }
    in_maps = [dict(weights, x_in=x[2 * c:2 * c + 2]) for c in range(NCORE)]
    res = run_bass_kernel_spmd(nc, in_maps, core_ids=list(range(NCORE)))
    return np.concatenate([r["y_out"] for r in res.results], axis=0)


# version 10 (final)
# speedup vs baseline: 1.0326x; 1.0326x over previous
import contextlib
import numpy as np
import ml_dtypes
import concourse.bass as bass
import concourse.mybir as mybir
from concourse.bass_utils import run_bass_kernel_spmd

F32 = mybir.dt.float32
BF16 = mybir.dt.bfloat16
I32 = mybir.dt.int32
ALU = mybir.AluOpType
ACT = mybir.ActivationFunctionType
AXX = mybir.AxisListType.X

S, D, H, HD, DL, HID = 2048, 1024, 16, 64, 256, 3072
NSEQ, NCORE, KB = 2, 8, 1024


def _finish(nc, P, blk_holder):
    blk_ctx = nc.Block()
    blk = blk_ctx.__enter__()

    @blk.sync
    def _(e):
        P.replay("sync", e, nc)

    @blk.gpsimd
    def _(e):
        P.replay("gpsimd", e, nc)

    @blk.vector
    def _(e):
        P.replay("vector", e, nc)

    @blk.scalar
    def _(e):
        P.replay("scalar", e, nc)

    @blk.tensor
    def _(e):
        P.replay("tensor", e, nc)

    blk_ctx.__exit__(None, None, None)
    return nc


class Ctr:
    def __init__(self, handle):
        self.h = handle
        self.n = 0


class Plan:
    ENGINES = ("sync", "gpsimd", "vector", "scalar", "tensor")

    def __init__(self):
        self.ops = {e: [] for e in self.ENGINES}
        self.scope = None

    def do(self, eng, fn, wait=(), inc=()):
        targets = [(c.h, v) for c, v in wait if v > 0]
        incs = []
        for c, k in inc:
            c.n += k
            incs.append((c.h, k))
        self.ops[eng].append((targets, fn, incs, self.scope))

    def replay(self, eng_name, handle, nc=None):
        for targets, fn, incs, scope in self.ops[eng_name]:
            ctx = (nc.named_scope(scope) if (scope and nc is not None)
                   else contextlib.nullcontext())
            with ctx:
                for h, v in targets:
                    handle.wait_ge(h, v)
                ins = fn(handle) if fn is not None else None
                for h, k in incs:
                    ins = ins.then_inc(h, k)


_LIM = {"seqs": 2, "pairs": 8, "heads": 2, "chunks": 4, "qmm": True, "mlim": 8, "qevac": True, "qnorm": True, "qn_stage": 3}


def build_program(upto="p3b", dbg_scratch=False):
    nc = bass.Bass("TRN2", target_bir_lowering=False, debug=False,
                   detect_race_conditions=False)

    x_in = nc.dram_tensor("x_in", [NSEQ, S, D], F32, kind="ExternalInput").ap()
    w_q = nc.dram_tensor("w_q", [D, D], F32, kind="ExternalInput").ap()
    w_k = nc.dram_tensor("w_k", [D, D], F32, kind="ExternalInput").ap()
    w_v = nc.dram_tensor("w_v", [D, D], F32, kind="ExternalInput").ap()
    w_o = nc.dram_tensor("w_o", [D, D], F32, kind="ExternalInput").ap()
    w_dn = nc.dram_tensor("w_dn", [D, DL], F32, kind="ExternalInput").ap()
    w_uk = nc.dram_tensor("w_uk", [DL, D], F32, kind="ExternalInput").ap()
    w_uv = nc.dram_tensor("w_uv", [DL, D], F32, kind="ExternalInput").ap()
    wg_i = nc.dram_tensor("wg_i", [HID, D], I32, kind="ExternalInput").ap()
    wu_i = nc.dram_tensor("wu_i", [HID, D], I32, kind="ExternalInput").ap()
    wd_i = nc.dram_tensor("wd_i", [D, HID], I32, kind="ExternalInput").ap()
    y_out = nc.dram_tensor("y_out", [NSEQ, S, D], F32, kind="ExternalOutput").ap()

    kd = "ExternalOutput" if dbg_scratch else "Internal"
    hs_hi = nc.dram_tensor("hs_hi", [NSEQ, H, HD, S], BF16, kind=kd).ap()
    hs_lo = nc.dram_tensor("hs_lo", [NSEQ, H, HD, S], BF16, kind=kd).ap()
    aw_d = nc.dram_tensor("aw_d", [128, 45056], BF16, kind=kd).ap()
    wo_hi_d = nc.dram_tensor("wo_hi_d", [128, 8192], BF16, kind=kd).ap()
    wo_lo_d = nc.dram_tensor("wo_lo_d", [128, 8192], BF16, kind=kd).ap()
    wgt_d = nc.dram_tensor("wgt_d", [D, HID], BF16, kind=kd).ap()
    wut_d = nc.dram_tensor("wut_d", [D, HID], BF16, kind=kd).ap()
    wdt_d = nc.dram_tensor("wdt_d", [HID, D], BF16, kind=kd).ap()
    y1_d = nc.dram_tensor("y1_d", [D, NSEQ * S], F32, kind=kd).ap()
    rden_d = nc.dram_tensor("rden_d", [1, S], F32, kind=kd).ap()
    bias_d = nc.dram_tensor("bias_d", [2, 1], F32, kind=kd).ap()
    if dbg_scratch:
        dbg_q = nc.dram_tensor("dbg_q", [128, H * S], BF16, kind=kd).ap()
        dbg_lh = nc.dram_tensor("dbg_lh", [128, 2 * S], BF16, kind=kd).ap()
        dbg_ll = nc.dram_tensor("dbg_ll", [128, 2 * S], BF16, kind=kd).ap()
        dbg_k = nc.dram_tensor("dbg_k", [128, 2 * S], BF16, kind=kd).ap()
        dbg_vh = nc.dram_tensor("dbg_vh", [128, 16 * 130], BF16, kind=kd).ap()
        dbg_vl = nc.dram_tensor("dbg_vl", [128, 16 * 130], BF16, kind=kd).ap()
        dbg_bias = nc.dram_tensor("dbg_bias", [128, 2], F32, kind=kd).ap()
        dbg_eh = nc.dram_tensor("dbg_eh", [128, S], BF16, kind=kd).ap()
        dbg_el = nc.dram_tensor("dbg_el", [128, S], BF16, kind=kd).ap()
        dbg_sd = nc.dram_tensor("dbg_sd", [128, S], F32, kind=kd).ap()
        dbg_dec = nc.dram_tensor("dbg_dec", [128, 2 * S], F32, kind=kd).ap()
        dbg_cw = nc.dram_tensor("dbg_cw", [128, 16], F32, kind=kd).ap()
        dbg_lam = nc.dram_tensor("dbg_lam", [128, 1], F32, kind=kd).ap()
        dbg_y1in = nc.dram_tensor("dbg_y1in", [128, 4096], F32, kind=kd).ap()
        dbg_y1h = nc.dram_tensor("dbg_y1h", [128, 4096], BF16, kind=kd).ap()
        dbg_y1l = nc.dram_tensor("dbg_y1l", [128, 4096], BF16, kind=kd).ap()
        dbg_hph = nc.dram_tensor("dbg_hph", [128, 12288], BF16, kind=kd).ap()
        dbg_hpl = nc.dram_tensor("dbg_hpl", [128, 12288], BF16, kind=kd).ap()
        dbg_zb = nc.dram_tensor("dbg_zb", [128, 4096], F32, kind=kd).ap()
        dbg_outf = nc.dram_tensor("dbg_outf", [128, 4096], F32, kind=kd).ap()
        dbg_di = nc.dram_tensor("dbg_di", [128, 2 * S], F32, kind=kd).ap()

    AW = {"wq_hi": 0, "wq_lo": 8192, "wkv_hi": 16384, "wkv_lo": 24576,
          "wdn_hi": 32768, "wdn_lo": 34816, "wuk_hi": 36864, "wuk_lo": 38912,
          "wuv_hi": 40960, "wuv_lo": 43008}

    eye_f32 = nc.inline_tensor(np.eye(128, dtype=np.float32), name="c_eye32").ap()
    eye_bf = nc.inline_tensor(np.eye(128).astype(ml_dtypes.bfloat16), name="c_eyebf").ap()
    hmask_c = nc.inline_tensor(
        np.stack([np.r_[np.ones(64), np.zeros(64)],
                  np.r_[np.zeros(64), np.ones(64)]], axis=1).astype(np.float32),
        name="c_hmask").ap()
    ones2_c = nc.inline_tensor(np.ones((128, 2), np.float32), name="c_ones2").ap()
    sel_c = nc.inline_tensor(
        np.stack([np.r_[np.ones(128), np.zeros(128)],
                  np.r_[np.zeros(128), np.ones(128)]]).astype(np.float32),
        name="c_sel").ap()
    o164_c = nc.inline_tensor(np.ones((1, 64), np.float32), name="c_o164").ap()
    o1128_c = nc.inline_tensor(np.ones((1, 128), np.float32), name="c_o1128").ap()
    init_c = nc.inline_tensor(
        np.tile(np.array([[100.0, 127.0, 100000.0, 0.0]], np.float32), (128, 1)),
        name="c_init").ap()
    _f = np.arange(2 * S, dtype=np.float64)[None, :]
    _p = np.arange(128, dtype=np.float64)[:, None]
    absdist_c = nc.inline_tensor(
        np.abs(_f - _p - S).astype(np.float32), name="c_absdist").ap()
    zero2_c = nc.inline_tensor(np.zeros((2, 8), np.float32), name="c_zero2").ap()
    vkh_pat = np.zeros((128, 16 * 130), np.float32)
    vkh_pat[:, 64::130] = 1.0
    vkh_pat[:, 129::130] = 1.0
    vkhi_c = nc.inline_tensor(vkh_pat.astype(ml_dtypes.bfloat16), name="c_vkhi").ap()
    vklo_c = nc.inline_tensor(np.zeros((128, 16 * 130)).astype(ml_dtypes.bfloat16),
                              name="c_vklo").ap()
    zero21_c = nc.inline_tensor(np.zeros((2, 1), np.float32), name="c_zero21").ap()

    P = Plan()
    blk_holder = []

    def sbuf(name, shape, dtype):
        return nc.alloc_sbuf_tensor(name, list(shape), dtype)

    eye32 = sbuf("eye32", [128, 128], F32)
    eyebf = sbuf("eyebf", [128, 128], BF16)
    hmask = sbuf("hmask", [128, 2], F32)
    ones2 = sbuf("ones2", [128, 2], F32)
    sel = sbuf("sel", [2, 256], F32)
    ones164 = sbuf("ones164", [1, 64], F32)
    ones1128 = sbuf("ones1128", [1, 128], F32)
    decay = sbuf("decay", [128, 2 * S], F32)
    lamn = sbuf("lamn", [128, 1], F32)
    sv = sbuf("sv", [128, 1], F32)
    s2v = sbuf("s2v", [128, 1], F32)
    epsv = sbuf("epsv", [128, 1], F32)
    cwork = sbuf("cwork", [128, 16], F32)
    qn2 = sbuf("qn2", [2, 8], F32)
    kn2 = sbuf("kn2", [2, 1], F32)
    biasv = sbuf("biasv", [128, 2], F32)

    AR = 188 * KB
    arena = nc.alloc_sbuf_tensor("arena", [128, AR], mybir.dt.uint8)
    ar_base = nc.lookup_mloc(arena).addr

    def at(name, off_kb, shape, dtype):
        per = int(np.prod(shape[1:])) * mybir.dt.size(dtype)
        off = int(off_kb * KB)
        assert off + per <= AR, (name, off_kb, per)
        return nc.alloc_sbuf_tensor_at(name, list(shape), dtype,
                                       offset=ar_base + off)

    wuk_hi = at("wuk_hi", 0, [128, 2 * KB], BF16)
    wuk_lo = at("wuk_lo", 4, [128, 2 * KB], BF16)
    wuv_hi = at("wuv_hi", 8, [128, 2 * KB], BF16)
    wuv_lo = at("wuv_lo", 12, [128, 2 * KB], BF16)
    qstk = at("qstk", 16, [128, H * S], BF16)
    lat_hi = at("lat_hi", 80, [128, 2 * S], BF16)
    lat_lo = at("lat_lo", 88, [128, 2 * S], BF16)
    stgA = at("stgA", 96, [128, KB], F32)
    stgB = at("stgB", 100, [128, KB], F32)
    stgC = at("stgC", 104, [128, KB], F32)
    shiA = at("shiA", 108, [128, KB], BF16)
    sloA = at("sloA", 110, [128, KB], BF16)
    shiB = at("shiB", 112, [128, KB], BF16)
    sloB = at("sloB", 114, [128, KB], BF16)
    wq_hi = at("wq_hi", 96, [128, 8 * KB], BF16)
    wq_lo = at("wq_lo", 112, [128, 8 * KB], BF16)
    xtokQ = [at("xtokQA", 128, [128, 4 * KB], F32),
             at("xtokQB", 144, [128, 4 * KB], F32)]
    xf_hi = at("xf_hi", 160, [128, 4 * KB], BF16)
    xf_lo = at("xf_lo", 168, [128, 4 * KB], BF16)
    sqb = at("sqb", 176, [128, 512], F32)
    wkv_hi = at("wkv_hi", 96, [128, 8 * KB], BF16)
    wkv_lo = at("wkv_lo", 112, [128, 8 * KB], BF16)
    wdn_hi = at("wdn_hi", 128, [128, 2 * KB], BF16)
    wdn_lo = at("wdn_lo", 132, [128, 2 * KB], BF16)
    xtokS = at("xtokS", 136, [128, 4 * KB], F32)
    xfk_hi = at("xfk_hi", 152, [128, 4 * KB], BF16)
    xfk_lo = at("xfk_lo", 160, [128, 4 * KB], BF16)
    kv_hi = at("kv_hi", 168, [128, 4 * KB], BF16)
    kv_lo = at("kv_lo", 176, [128, 4 * KB], BF16)
    dint = at("dint", 128, [128, 2 * S], I32)
    kstkA0 = at("kstkA0", 96, [128, S], BF16)
    kstkA1 = at("kstkA1", 100, [128, S], BF16)
    kstkB0 = at("kstkB0", 104, [128, S], BF16)
    kstkB1 = at("kstkB1", 108, [128, S], BF16)
    vk_hi = at("vk_hi", 112, [128, 16 * 130], BF16)
    vk_lo = at("vk_lo", 116.25, [128, 16 * 130], BF16)
    sdec = [at("sdec0", 120.5, [128, S], F32), at("sdec1", 128.5, [128, S], F32)]
    Ebuf = [at("Ebuf0", 136.5, [128, S], F32), at("Ebuf1", 144.5, [128, S], F32)]
    ehi3 = [at(f"ehi{i}", 152.5 + 4 * i, [128, S], BF16) for i in range(3)]
    elo3 = [at(f"elo{i}", 164.5 + 4 * i, [128, S], BF16) for i in range(3)]
    hssb_hi = at("hssb_hi", 176.5, [64, S], BF16)
    hssb_lo = at("hssb_lo", 180.5, [64, S], BF16)
    rden = at("rden", 176.5, [1, S], F32)
    rb = at("rb", 176.5, [64, S], F32)
    sqb2 = at("sqb2", 184.5, [128, 512], F32)
    att_hi2 = [at("att_hiA", 0, [128, 4 * KB], BF16),
               at("att_hiB", 118, [128, 4 * KB], BF16)]
    att_lo2 = [at("att_loA", 8, [128, 4 * KB], BF16),
               at("att_loB", 126, [128, 4 * KB], BF16)]
    x3tok = [at("x3tokA", 16, [128, 2 * KB], F32),
             at("x3tokB", 24, [128, 2 * KB], F32)]
    xf3 = at("xf3", 32, [128, 4 * KB], F32)
    r1b = at("r1b", 48, [128, 4 * KB], F32)
    y1b = at("y1b", 64, [128, 4 * KB], F32)
    wo_hi = at("wo_hi", 80, [128, 8 * KB], BF16)
    wo_lo = at("wo_lo", 96, [128, 8 * KB], BF16)
    sq3 = at("sq3", 112, [128, 512], F32)
    mb3 = at("mb3", 114, [128, 512], F32)
    rstdb3 = at("rstdb3", 116, [128, 512], F32)
    wgsb = at("wgsb", 0, [128, 24 * KB], BF16)
    wusb = at("wusb", 48, [128, 24 * KB], BF16)
    hph = at("hph", 96, [128, 12 * KB], BF16)
    hpl = at("hpl", 120, [128, 12 * KB], BF16)
    y1sb = at("y1sb", 144, [128, 4 * KB], F32)
    y1h = at("y1h", 160, [128, 4 * KB], BF16)
    y1l = at("y1l", 168, [128, 4 * KB], BF16)
    wdsb = [at("wdsbA", 176, [128, KB], BF16), at("wdsbB", 178, [128, KB], BF16)]
    silb = at("silb", 180, [128, 512], F32)
    hpb = at("hpb", 182, [128, 512], F32)
    dbuf = [at("dbufA", 184, [128, 512], F32), at("dbufB", 186, [128, 512], F32)]
    outf = at("outf", 96, [128, 4 * KB], F32)
    outtok = at("outtok", 112, [128, 4 * KB], F32)
    zsq = at("zsq", 128, [128, 512], F32)
    mbf = at("mbf", 130, [128, 512], F32)
    rstdbf = at("rstdbf", 132, [128, 512], F32)

    psa = [nc.alloc_psum_tensor(f"psa{i}", [128, 512], F32) for i in range(4)]
    psb = [nc.alloc_psum_tensor(f"psb{i}", [128, 512], F32) for i in range(4)]

    dl = Ctr(nc.alloc_semaphore("s_dl"))
    dg = Ctr(nc.alloc_semaphore("s_dg"))
    ds = Ctr(nc.alloc_semaphore("s_ds"))
    te = Ctr(nc.alloc_semaphore("s_te"))
    ve = Ctr(nc.alloc_semaphore("s_ve"))
    se = Ctr(nc.alloc_semaphore("s_se"))
    gp = Ctr(nc.alloc_semaphore("s_gp"))
    QSEM = {"sync": dl, "gpsimd": dg, "scalar": ds}
    swd = [Ctr(nc.alloc_semaphore("s_wd0")), Ctr(nc.alloc_semaphore("s_wd1"))]
    swg = [Ctr(nc.alloc_semaphore("s_wg0")), Ctr(nc.alloc_semaphore("s_wg1"))]
    satt = [Ctr(nc.alloc_semaphore("s_at0")), Ctr(nc.alloc_semaphore("s_at1"))]
    srd = Ctr(nc.alloc_semaphore("s_rd"))
    sbias = Ctr(nc.alloc_semaphore("s_bias"))

    def dma(q, dst, src, wait=(), ctr=None):
        c = ctr if ctr is not None else QSEM[q]
        P.do(q, lambda e, d=dst, s=src: e.dma_start(out=d, in_=s),
             wait=wait, inc=[(c, 16)])
        return (c, c.n)

    def split_to(v, hi_ap, lo_ap, src_ap):
        v.tensor_copy(hi_ap, src_ap)
        return v.tensor_sub(lo_ap, src_ap, hi_ap)

    P.scope = "p0"
    c_eye = dma("sync", eye32[:, :], eye_f32)
    dma("sync", eyebf[:, :], eye_bf)
    dma("sync", hmask[:, :], hmask_c)
    dma("sync", ones2[:, :], ones2_c)
    dma("sync", sel[:, :], sel_c)
    dma("sync", ones164[:, :], o164_c)
    const_ld = dma("sync", ones1128[:, :], o1128_c)
    init_ld = dma("sync", cwork[:, 0:4], init_c)

    def emit_recips(v):
        v.reciprocal(cwork[:, 4:5], cwork[:, 0:1])
        v.reciprocal(sv[:, :], cwork[:, 1:2])
        return v.reciprocal(epsv[:, :], cwork[:, 2:3])
    P.do("vector", emit_recips, wait=[init_ld], inc=[(ve, 1)])
    recips_done = (ve, ve.n)

    def emit_consts(v):
        v.tensor_scalar_mul(lamn[:, :], cwork[:, 4:5], -1.0)
        return v.tensor_mul(s2v[:, :], sv[:, :], sv[:, :])
    P.do("vector", emit_consts, wait=[recips_done], inc=[(ve, 1)])
    consts_done = (ve, ve.n)

    absd_ld = dma("sync", dint[:, :].bitcast(F32), absdist_c)
    if dbg_scratch:
        absd_ld = dma("scalar", dbg_di, dint[:, :].bitcast(F32), wait=[absd_ld])
    P.do("scalar", lambda s_: s_.activation(decay[:, :], dint[:, :].bitcast(F32),
                                            ACT.Exp, bias=0.0, scale=lamn[:, :]),
         wait=[absd_ld, consts_done], inc=[(se, 1)])
    decay_done = (se, se.n)

    def ffn_transpose(src_i32, n_r, n_c, row_w, dst_d):
        casts = []
        first_w = [(te, te.n)]
        for r in range(n_r):
            casts.append(dma("gpsimd", wgsb[:, r * row_w:(r + 1) * row_w],
                             src_i32[r * 128:(r + 1) * 128, :],
                             wait=(first_w if r == 0 else [])))
        all_cast = casts[-1]
        strip_st = None
        for c in range(n_c):
            ev_tail = None
            for r4 in range(0, n_r, 4):
                nn = min(4, n_r - r4)
                bank = (r4 // 4) % 2
                for i in range(nn):
                    r = r4 + i
                    if r4 == 0 and i == 0:
                        w0 = [all_cast, const_ld, (ve, ve.n)]
                    elif i == 0:
                        w0 = [(ve, ve.n)]
                    else:
                        w0 = []
                    P.do("tensor", lambda t, r=r, c=c, bank=bank, i=i, w=row_w:
                         t.transpose(psb[bank][:, i * 64:(i + 1) * 64].bitcast(BF16),
                                     wgsb[:, r * w + c * 128:r * w + (c + 1) * 128],
                                     eyebf[:, :]),
                         wait=w0, inc=([(te, 1)] if i == nn - 1 else []))
                tp_done = (te, te.n)
                ws = [tp_done] + ([strip_st] if (r4 == 0 and strip_st) else [])
                P.do("vector", lambda v, r4=r4, nn=nn, bank=bank:
                     v.tensor_copy(wusb[:, r4 * 128:(r4 + nn) * 128],
                                   psb[bank][:, 0:nn * 64].bitcast(BF16)),
                     wait=ws, inc=[(ve, 1)])
                ev_tail = (ve, ve.n)
            strip_st = dma("scalar", dst_d[c * 128:(c + 1) * 128, :],
                           wusb[:, 0:n_r * 128], wait=[ev_tail])
        return strip_st

    ffn_transpose(wg_i, HID // 128, D // 128, D, wgt_d)
    ffn_transpose(wu_i, HID // 128, D // 128, D, wut_d)
    ffn_transpose(wd_i, D // 128, HID // 128, HID, wdt_d)
    ffnw_done = (ds, ds.n)

    prev_split = [None, None]
    prev_store = [None, None]

    def w_chunk(par, srcs, hi_dst, lo_dst, cols):
        stg = stgA if par == 0 else stgB
        shi = shiA if par == 0 else shiB
        slo = sloA if par == 0 else sloB
        w0 = [prev_split[par]] if prev_split[par] else []
        dma("sync", stg[:, 0:cols], srcs[0], wait=w0)
        if len(srcs) > 1:
            dma("sync", stgC[:, 0:cols], srcs[1])
        ld_t = (dl, dl.n)

        def fn(v, stg=stg, shi=shi, slo=slo, cols=cols, two=len(srcs) > 1):
            if two:
                v.tensor_add(stg[:, 0:cols], stg[:, 0:cols], stgC[:, 0:cols])
            return split_to(v, shi[:, 0:cols], slo[:, 0:cols], stg[:, 0:cols])
        ws = [ld_t] + ([prev_store[par]] if prev_store[par] else [])
        P.do("vector", fn, wait=ws, inc=[(ve, 1)])
        prev_split[par] = (ve, ve.n)
        dma("sync", hi_dst, shi[:, 0:cols], wait=[prev_split[par]])
        prev_store[par] = dma("sync", lo_dst, slo[:, 0:cols],
                              wait=[prev_split[par]])

    for r in range(8):
        w_chunk(r % 2, [w_q[r * 128:(r + 1) * 128, :]],
                aw_d[:, AW["wq_hi"] + r * D:AW["wq_hi"] + (r + 1) * D],
                aw_d[:, AW["wq_lo"] + r * D:AW["wq_lo"] + (r + 1) * D], D)
    for r in range(8):
        w_chunk(r % 2, [w_k[r * 128:(r + 1) * 128, :], w_v[r * 128:(r + 1) * 128, :]],
                aw_d[:, AW["wkv_hi"] + r * D:AW["wkv_hi"] + (r + 1) * D],
                aw_d[:, AW["wkv_lo"] + r * D:AW["wkv_lo"] + (r + 1) * D], D)
    for r in range(8):
        w_chunk(r % 2, [w_dn[r * 128:(r + 1) * 128, :]],
                aw_d[:, AW["wdn_hi"] + r * DL:AW["wdn_hi"] + (r + 1) * DL],
                aw_d[:, AW["wdn_lo"] + r * DL:AW["wdn_lo"] + (r + 1) * DL], DL)
    for r in range(2):
        w_chunk(r % 2, [w_uk[r * 128:(r + 1) * 128, :]],
                aw_d[:, AW["wuk_hi"] + r * D:AW["wuk_hi"] + (r + 1) * D],
                aw_d[:, AW["wuk_lo"] + r * D:AW["wuk_lo"] + (r + 1) * D], D)
    for r in range(2):
        w_chunk(r % 2, [w_uv[r * 128:(r + 1) * 128, :]],
                aw_d[:, AW["wuv_hi"] + r * D:AW["wuv_hi"] + (r + 1) * D],
                aw_d[:, AW["wuv_lo"] + r * D:AW["wuv_lo"] + (r + 1) * D], D)
    for r in range(8):
        w_chunk(r % 2, [w_o[r * 128:(r + 1) * 128, :]],
                wo_hi_d[:, r * D:(r + 1) * D], wo_lo_d[:, r * D:(r + 1) * D], D)
    aw_done_ve = (ve, ve.n)
    aw_store_done = (dl, dl.n)

    dma("sync", wuk_hi[:, :], aw_d[:, AW["wuk_hi"]:AW["wuk_hi"] + 2 * KB],
        wait=[aw_store_done])
    dma("sync", wuk_lo[:, :], aw_d[:, AW["wuk_lo"]:AW["wuk_lo"] + 2 * KB])
    dma("sync", wuv_hi[:, :], aw_d[:, AW["wuv_hi"]:AW["wuv_hi"] + 2 * KB])
    wu_res = dma("sync", wuv_lo[:, :], aw_d[:, AW["wuv_lo"]:AW["wuv_lo"] + 2 * KB])

    if upto == "p0":
        P.do("sync", None, wait=[wu_res])
        return _finish(nc, P, blk_holder)
    TCH = 512
    NCH = S // TCH
    hs_st_tail = None
    p2_tail_ve, p2_tail_te, p2_tail_se = None, None, None

    for s in range(_LIM["seqs"]):
        P.scope = "p1q_s%d" % s
        gate_w = [(te, te.n), (ve, ve.n), aw_store_done]
        dma("sync", wq_hi[:, :], aw_d[:, AW["wq_hi"]:AW["wq_hi"] + 8 * KB],
            wait=gate_w)
        wq_ld = dma("sync", wq_lo[:, :], aw_d[:, AW["wq_lo"]:AW["wq_lo"] + 8 * KB])

        qn_init = dma("sync", qn2[:, :], zero2_c, wait=gate_w)

        tp_tail = {}
        tp_tail[-2] = decay_done
        tp_tail[-1] = decay_done
        q_bank_free = [None, None]

        def qevac_fn(v, m, bank, chv):
            h0, h1 = 2 * m, 2 * m + 1
            c0 = chv * TCH
            p0 = psa[bank][0:64, 0:TCH]
            p1 = psa[bank][64:128, 0:TCH]
            v.tensor_copy(qstk[0:64, h0 * S + c0:h0 * S + c0 + TCH], p0)
            v.tensor_sub(qstk[64:128, h0 * S + c0:h0 * S + c0 + TCH], p0,
                         qstk[0:64, h0 * S + c0:h0 * S + c0 + TCH])
            v.tensor_copy(qstk[0:64, h1 * S + c0:h1 * S + c0 + TCH], p1)
            return v.tensor_sub(qstk[64:128, h1 * S + c0:h1 * S + c0 + TCH],
                                p1, qstk[0:64, h1 * S + c0:h1 * S + c0 + TCH])
        for chv in range(_LIM["chunks"]):
            buf = xtokQ[chv % 2]
            lw = [wq_ld] + ([tp_tail[chv - 2]] if chv - 2 in tp_tail else [])
            xl = dma("sync", buf[:, :].rearrange("p (a d) -> p a d", a=4),
                     x_in[s, chv * TCH:(chv + 1) * TCH, :]
                     .rearrange("(a p) d -> p a d", p=128), wait=lw)
            for f in range(8):
                bank = f % 2
                for tt in range(4):
                    if f == 0 and tt == 0:
                        w0 = [xl, wq_ld, (ve, ve.n)]
                    elif tt == 0:
                        w0 = [(ve, ve.n)]
                    else:
                        w0 = []
                    P.do("tensor", lambda t, f=f, tt=tt, bank=bank, b=buf:
                         t.transpose(psb[bank][:, tt * 128:(tt + 1) * 128],
                                     b[:, tt * D + f * 128:tt * D + (f + 1) * 128],
                                     eye32[:, :]),
                         wait=w0, inc=([(te, 1)] if tt == 3 else []))
                tpd = (te, te.n)
                P.do("vector", lambda v, f=f, bank=bank: split_to(
                    v, xf_hi[:, f * TCH:(f + 1) * TCH],
                    xf_lo[:, f * TCH:(f + 1) * TCH],
                    psb[bank][:, 0:TCH]),
                    wait=[tpd], inc=[(ve, 1)])
            tp_tail[chv] = (te, te.n)
            xf_done = (ve, ve.n)

            if not _LIM["qmm"]:
                continue
            for m in range(_LIM["mlim"]):
                bank = m % 2
                terms = []
                for r in range(8):
                    terms += [(wq_hi, xf_hi, r), (wq_hi, xf_lo, r), (wq_lo, xf_hi, r)]
                for ti, (lh, rh, r) in enumerate(terms):
                    w0 = []
                    if ti == 0:
                        w0 = [xf_done]
                        if q_bank_free[bank]:
                            w0.append(q_bank_free[bank])
                    P.do("tensor", lambda t, m=m, r=r, lh=lh, rh=rh, bank=bank,
                         first=(ti == 0), last=(ti == len(terms) - 1):
                         t.matmul(psa[bank][:, 0:TCH],
                                  lh[:, r * D + m * 128:r * D + (m + 1) * 128],
                                  rh[:, r * TCH:(r + 1) * TCH],
                                  start=first, stop=last),
                         wait=w0, inc=([(te, 1)] if ti == len(terms) - 1 else []))
                q_mm = (te, te.n)
                if not _LIM["qnorm"]:
                    if _LIM["qevac"]:
                        P.do("vector", lambda v, m=m, bank=bank, chv=chv: qevac_fn(
                            v, m, bank, chv), wait=[q_mm], inc=[(ve, 1)])
                        q_bank_free[bank] = (ve, ve.n)
                    else:
                        P.do("vector", lambda v, bank=bank: v.tensor_copy(
                            sqb[:, :], psa[bank][:, 0:TCH]), wait=[q_mm],
                            inc=[(ve, 1)])
                        q_bank_free[bank] = (ve, ve.n)
                    continue
                P.do("vector", lambda v, m=m, bank=bank, chv=chv: qevac_fn(
                    v, m, bank, chv), wait=[q_mm, qn_init], inc=[(ve, 1)])
                qe_t = (ve, ve.n)

                if _LIM["qn_stage"] >= 1:
                    P.do("scalar", lambda sc, bank=bank:
                         sc.activation(sqb[:, :], psa[bank][:, 0:TCH],
                                       ACT.Square), wait=[q_mm, qe_t],
                         inc=[(se, 1)])
                    sq_done = (se, se.n)
                    q_bank_free[bank] = (se, se.n)
                else:
                    q_bank_free[bank] = qe_t

                if _LIM["qn_stage"] >= 2:
                    P.do("tensor", lambda t: t.matmul(
                        psb[2][0:2, 0:TCH], hmask[:, :], sqb[:, :],
                        start=True, stop=True),
                        wait=[sq_done, (ve, ve.n)], inc=[(te, 1)])
                    nrm_mm = (te, te.n)

                if _LIM["qn_stage"] >= 3:
                    def qn_acc(v, m=m):
                        v.reduce_max(cwork[0:2, 5:6], psb[2][0:2, 0:TCH], axis=AXX)
                        return v.tensor_max(qn2[0:2, m:m + 1], qn2[0:2, m:m + 1],
                                            cwork[0:2, 5:6])
                    P.do("vector", qn_acc, wait=[nrm_mm, qn_init], inc=[(ve, 1)])
        passq_te = (te, te.n)
        passq_ve = (ve, ve.n)

        P.scope = "p1kv_s%d" % s
        if upto == "p1q":
            continue
        dma("sync", wkv_hi[:, :], aw_d[:, AW["wkv_hi"]:AW["wkv_hi"] + 8 * KB],
            wait=[passq_te, passq_ve])
        dma("sync", wkv_lo[:, :], aw_d[:, AW["wkv_lo"]:AW["wkv_lo"] + 8 * KB])
        dma("sync", wdn_hi[:, :], aw_d[:, AW["wdn_hi"]:AW["wdn_hi"] + 2 * KB])
        wkv_ld = dma("sync", wdn_lo[:, :], aw_d[:, AW["wdn_lo"]:AW["wdn_lo"] + 2 * KB])

        kv_bank_free = [None, None]
        lat_free = None
        prev_tp = None
        for chv in range(NCH):
            lw = [wkv_ld] + ([prev_tp] if prev_tp else [])
            xl = dma("sync", xtokS[:, :].rearrange("p (a d) -> p a d", a=4),
                     x_in[s, chv * TCH:(chv + 1) * TCH, :]
                     .rearrange("(a p) d -> p a d", p=128), wait=lw)
            for f in range(8):
                bank = f % 2
                for tt in range(4):
                    if f == 0 and tt == 0:
                        w0 = [xl, (ve, ve.n)]
                    elif tt == 0:
                        w0 = [(ve, ve.n)]
                    else:
                        w0 = []
                    P.do("tensor", lambda t, f=f, tt=tt, bank=bank:
                         t.transpose(psb[bank][:, tt * 128:(tt + 1) * 128],
                                     xtokS[:, tt * D + f * 128:tt * D + (f + 1) * 128],
                                     eye32[:, :]),
                         wait=w0, inc=([(te, 1)] if tt == 3 else []))
                tpd = (te, te.n)
                P.do("vector", lambda v, f=f, bank=bank: split_to(
                    v, xfk_hi[:, f * TCH:(f + 1) * TCH],
                    xfk_lo[:, f * TCH:(f + 1) * TCH],
                    psb[bank][:, 0:TCH]),
                    wait=[tpd], inc=[(ve, 1)])
            prev_tp = (te, te.n)
            xf_done = (ve, ve.n)

            for m in range(8):
                bank = m % 2
                terms = []
                for r in range(8):
                    terms += [(wkv_hi, xfk_hi, r), (wkv_hi, xfk_lo, r),
                              (wkv_lo, xfk_hi, r)]
                for ti, (lh, rh, r) in enumerate(terms):
                    w0 = []
                    if ti == 0:
                        w0 = [xf_done]
                        if kv_bank_free[bank]:
                            w0.append(kv_bank_free[bank])
                    P.do("tensor", lambda t, m=m, r=r, lh=lh, rh=rh, bank=bank,
                         first=(ti == 0), last=(ti == len(terms) - 1):
                         t.matmul(psa[bank][:, 0:TCH],
                                  lh[:, r * D + m * 128:r * D + (m + 1) * 128],
                                  rh[:, r * TCH:(r + 1) * TCH],
                                  start=first, stop=last),
                         wait=w0, inc=([(te, 1)] if ti == len(terms) - 1 else []))
                kv_mm = (te, te.n)
                P.do("vector", lambda v, m=m, bank=bank: split_to(
                    v, kv_hi[:, m * TCH:(m + 1) * TCH],
                    kv_lo[:, m * TCH:(m + 1) * TCH],
                    psa[bank][:, 0:TCH]),
                    wait=[kv_mm], inc=[(ve, 1)])
                kv_bank_free[bank] = (ve, ve.n)
            kv_all = (ve, ve.n)

            for lm in range(2):
                terms = []
                for r in range(8):
                    terms += [(wdn_hi, kv_hi, r), (wdn_hi, kv_lo, r),
                              (wdn_lo, kv_hi, r)]
                for ti, (lh, rh, r) in enumerate(terms):
                    w0 = []
                    if ti == 0:
                        w0 = [kv_all]
                        if lat_free:
                            w0.append(lat_free)
                    P.do("tensor", lambda t, lm=lm, r=r, lh=lh, rh=rh,
                         first=(ti == 0), last=(ti == len(terms) - 1):
                         t.matmul(psb[3][:, 0:TCH],
                                  lh[:, r * DL + lm * 128:r * DL + (lm + 1) * 128],
                                  rh[:, r * TCH:(r + 1) * TCH],
                                  start=first, stop=last),
                         wait=w0, inc=([(te, 1)] if ti == len(terms) - 1 else []))
                lat_mm = (te, te.n)
                P.do("vector", lambda v, lm=lm, chv=chv: split_to(
                    v, lat_hi[:, lm * S + chv * TCH:lm * S + (chv + 1) * TCH],
                    lat_lo[:, lm * S + chv * TCH:lm * S + (chv + 1) * TCH],
                    psb[3][:, 0:TCH]),
                    wait=[lat_mm], inc=[(ve, 1)])
                lat_free = (ve, ve.n)
        p1_ve = (ve, ve.n)
        p1_te = (te, te.n)
        if dbg_scratch and s == 0:
            dma("scalar", dbg_q, qstk[:, :], wait=[p1_ve])
            dma("scalar", dbg_lh, lat_hi[:, :], wait=[p1_ve])
            dma("scalar", dbg_ll, lat_lo[:, :], wait=[p1_ve])
            dma("scalar", dbg_dec, decay[:, :], wait=[p1_ve])
            dma("scalar", dbg_cw, cwork[:, :], wait=[p1_ve])
            dma("scalar", dbg_lam, lamn[:, :], wait=[p1_ve])

        P.scope = "p2_s%d" % s
        if upto in ("p1", "p1q"):
            continue

        dma("sync", vk_hi[:, :], vkhi_c, wait=[p1_ve, wu_res])
        vk_ones_done = dma("sync", vk_lo[:, :], vklo_c)

        xh = {"dk15": (ve, ve.n), "hsplit": (ve, ve.n), "pv15": (te, te.n),
              "exp14": (se, se.n), "exp15": (se, se.n),
              "ehi14": (ve, ve.n), "ehi15": (ve, ve.n),
              "gp14": (gp, gp.n), "gp15": (gp, gp.n),
              "vel14": (ve, ve.n), "vel15": (ve, ve.n)}
        pending_tail = [None]
        for pair in range(_LIM["pairs"]):
            kn_init = dma("sync", kn2[:, :], zero21_c, wait=[(ve, ve.n)])
            kevac_t = {}
            kn_acc_prev = None
            for nb in range(4):
                bank = nb % 2
                terms = []
                for r in range(2):
                    terms += [(wuk_hi, lat_hi, r), (wuk_hi, lat_lo, r),
                              (wuk_lo, lat_hi, r)]
                for ti, (lh, rh, r) in enumerate(terms):
                    w0 = ([p1_ve, xh["dk15"]] +
                          ([kevac_t[nb - 2]] if nb - 2 in kevac_t else [])
                          ) if ti == 0 else []
                    P.do("tensor", lambda t, lh=lh, rh=rh, r=r, nb=nb, bank=bank,
                         pair=pair, first=(ti == 0), last=(ti == len(terms) - 1):
                         t.matmul(psa[bank][:, :],
                                  lh[:, r * D + pair * 128:r * D + (pair + 1) * 128],
                                  rh[:, r * S + nb * 512:r * S + (nb + 1) * 512],
                                  start=first, stop=last),
                         wait=w0, inc=([(te, 1)] if ti == len(terms) - 1 else []))
                k_mm = (te, te.n)

                def kevac(v, nb=nb, bank=bank):
                    c0 = nb * 512
                    p0 = psa[bank][0:64, :]
                    p1 = psa[bank][64:128, :]
                    v.tensor_copy(kstkA0[0:64, c0:c0 + 512], p0)
                    v.tensor_sub(kstkA0[64:128, c0:c0 + 512], p0,
                                 kstkA0[0:64, c0:c0 + 512])
                    v.tensor_copy(kstkA1[0:64, c0:c0 + 512], p1)
                    v.tensor_sub(kstkA1[64:128, c0:c0 + 512], p1,
                                 kstkA1[0:64, c0:c0 + 512])
                    v.tensor_copy(kstkB0[0:64, c0:c0 + 512],
                                  kstkA0[64:128, c0:c0 + 512])
                    v.tensor_copy(kstkB0[64:128, c0:c0 + 512],
                                  kstkA0[0:64, c0:c0 + 512])
                    v.tensor_copy(kstkB1[0:64, c0:c0 + 512],
                                  kstkA1[64:128, c0:c0 + 512])
                    return v.tensor_copy(kstkB1[64:128, c0:c0 + 512],
                                         kstkA1[0:64, c0:c0 + 512])
                P.do("vector", kevac, wait=[k_mm], inc=[(ve, 1)])
                kevac_t[nb] = (ve, ve.n)
                P.do("scalar", lambda sc, bank=bank:
                     sc.activation(sqb2[:, :], psa[bank][:, :],
                                   ACT.Square), wait=[k_mm, kevac_t[nb]],
                     inc=[(se, 1)])
                ksq = (se, se.n)
                kevac_t[nb] = ksq
                P.do("tensor", lambda t: t.matmul(
                    psa[2][0:2, :], hmask[:, :], sqb2[:, :],
                    start=True, stop=True),
                    wait=[ksq, kn_init, xh["dk15"]] +
                         ([kn_acc_prev] if kn_acc_prev else []),
                    inc=[(te, 1)])
                kn_mm = (te, te.n)

                def kn_acc(v):
                    v.reduce_max(cwork[0:2, 6:7], psa[2][0:2, :], axis=AXX)
                    return v.tensor_max(kn2[:, :], kn2[:, :], cwork[0:2, 6:7])
                P.do("vector", kn_acc, wait=[kn_mm, kn_init], inc=[(ve, 1)])
                kn_acc_prev = (ve, ve.n)
            kn_done = (ve, ve.n)

            P.do("vector", lambda v, pair=pair: v.tensor_mul(
                cwork[0:2, 7:8], qn2[0:2, pair:pair + 1], kn2[:, :]),
                wait=[kn_done, (se, se.n)], inc=[(ve, 1)])
            bp = (ve, ve.n)
            P.do("scalar", lambda sc: sc.activation(
                cwork[0:2, 8:9], cwork[0:2, 7:8], ACT.Sqrt),
                wait=[bp], inc=[(se, 1)])
            bsq = (se, se.n)
            P.do("vector", lambda v: v.tensor_scalar_mul(
                cwork[0:2, 9:10], cwork[0:2, 8:9], -1.05 / 32.0),
                wait=[bsq], inc=[(ve, 1)])
            bneg = (ve, ve.n)
            b_st = dma("gpsimd", bias_d, cwork[0:2, 9:10], wait=[bneg])
            bias_view, _ = bass.broadcast_tensor_aps(
                bias_d.rearrange("a b -> b a"), biasv[:, 0:2])
            bias_done = dma("gpsimd", biasv[:, 0:2], bias_view,
                            wait=[b_st, (se, se.n)], ctr=sbias)

            vevac_t = {}
            for tt in range(16):
                bank = 2 + tt % 2
                terms = []
                for r in range(2):
                    terms += [(lat_hi, wuv_hi, r), (lat_lo, wuv_hi, r),
                              (lat_hi, wuv_lo, r)]
                for ti, (lh, rh, r) in enumerate(terms):
                    w0 = ([p1_ve, xh["dk15"], vk_ones_done, kn_done] +
                          ([vevac_t[tt - 2]] if tt - 2 in vevac_t else [])
                          ) if ti == 0 else []
                    P.do("tensor", lambda t, lh=lh, rh=rh, r=r, tt=tt, bank=bank,
                         pair=pair, first=(ti == 0), last=(ti == len(terms) - 1):
                         t.matmul(psa[bank][:, 0:128],
                                  lh[:, r * S + tt * 128:r * S + (tt + 1) * 128],
                                  rh[:, r * D + pair * 128:r * D + (pair + 1) * 128],
                                  start=first, stop=last),
                         wait=w0, inc=([(te, 1)] if ti == len(terms) - 1 else []))
                v_mm = (te, te.n)

                def vevac(v, tt=tt, bank=bank):
                    c0 = tt * 130
                    pa = psa[bank][:, 0:64]
                    pb = psa[bank][:, 64:128]
                    v.tensor_copy(vk_hi[:, c0:c0 + 64], pa)
                    v.tensor_sub(vk_lo[:, c0:c0 + 64], pa, vk_hi[:, c0:c0 + 64])
                    v.tensor_copy(vk_hi[:, c0 + 65:c0 + 129], pb)
                    return v.tensor_sub(vk_lo[:, c0 + 65:c0 + 129], pb,
                                        vk_hi[:, c0 + 65:c0 + 129])
                P.do("vector", vevac, wait=[v_mm], inc=[(ve, 1)])
                vevac_t[tt] = (ve, ve.n)
            vk_done = (ve, ve.n)
            if dbg_scratch and s == 0 and pair == 0:
                dma("scalar", dbg_k[:, 0:S], kstkA0[:, :], wait=[vk_done])
                dma("scalar", dbg_k[:, S:2 * S], kstkB0[:, :], wait=[vk_done])
                dma("scalar", dbg_vh, vk_hi[:, :], wait=[vk_done])
                dma("scalar", dbg_vl, vk_lo[:, :], wait=[vk_done])
                dma("scalar", dbg_bias, biasv[:, :], wait=[vk_done, bias_done])

            for hh in range(_LIM["heads"]):
                hd = 2 * pair + hh
                kA = kstkA0 if hh == 0 else kstkA1
                kB = kstkB0 if hh == 0 else kstkB1
                te_scores_b, te_pv = {}, {}
                dk_b, se_exp = {}, {}
                ve_ehi, gp_elo, ve_elo = {}, {}, {}

                def emit_scores(jt):
                    for ib in range(4):
                        for pi, kk in ((0, kA), (1, kB)):
                            if pi == 0 and jt > 0:
                                w0 = [dk_b[jt - 1][ib]]
                            elif pi == 0 and ib == 0:
                                w0 = [vk_done, bias_done, xh["dk15"]]
                            else:
                                w0 = []
                            P.do("tensor", lambda t, jt=jt, kk=kk, ib=ib, pi=pi,
                                 hd=hd:
                                 t.matmul(psa[ib][:, :],
                                          kk[:, jt * 128:(jt + 1) * 128],
                                          qstk[:, hd * S + ib * 512:
                                               hd * S + (ib + 1) * 512],
                                          start=(pi == 0), stop=(pi == 1)),
                                 wait=w0,
                                 inc=([(te, 1)] if pi == 1 else []))
                        te_scores_b[(jt, ib)] = (te, te.n)

                def emit_softmax(jt):
                    sd = sdec[jt % 2]
                    Eb = Ebuf[jt % 2]
                    eh, el = ehi3[jt % 3], elo3[jt % 3]
                    dk_b[jt] = []
                    for ib in range(4):
                        w0 = [te_scores_b[(jt, ib)]]
                        if ib == 0:
                            if jt >= 2:
                                w0.append(se_exp[jt - 2])
                            else:
                                w0.append(xh["exp14" if jt == 0 else "exp15"])
                        P.do("vector", lambda v, jt=jt, sd=sd, ib=ib:
                             v.tensor_mul(sd[:, ib * 512:(ib + 1) * 512],
                                          psa[ib][:, :],
                                          decay[:, S - jt * 128 + ib * 512:
                                                S - jt * 128 + (ib + 1) * 512]),
                             wait=w0, inc=[(ve, 1)])
                        dk_b[jt].append((ve, ve.n))
                    w1 = [dk_b[jt][3]]
                    if jt >= 2:
                        w1 += [gp_elo[jt - 2], ve_elo[jt - 2], ve_ehi[jt - 2]]
                    else:
                        sfx = "14" if jt == 0 else "15"
                        w1 += [xh["gp" + sfx], xh["vel" + sfx], xh["ehi" + sfx]]
                    P.do("scalar", lambda sc, sd=sd, Eb=Eb, hh=hh:
                         sc.activation(Eb[:, :], sd[:, :], ACT.Exp,
                                       bias=biasv[:, hh:hh + 1], scale=0.03125),
                         wait=w1, inc=[(se, 1)])
                    se_exp[jt] = (se, se.n)
                    w2 = [se_exp[jt]]
                    w2.append(te_pv[jt - 3] if jt >= 3 else xh["pv15"])
                    P.do("vector", lambda v, Eb=Eb, eh=eh:
                         v.tensor_copy(eh[:, :], Eb[:, :]),
                         wait=w2, inc=[(ve, 1)])
                    ve_ehi[jt] = (ve, ve.n)
                    w3 = [ve_ehi[jt]]
                    w3.append(te_pv[jt - 3] if jt >= 3 else xh["pv15"])
                    P.do("gpsimd", lambda g, Eb=Eb, eh=eh, el=el:
                         g.tensor_sub(el[:, 0:1792], Eb[:, 0:1792],
                                      eh[:, 0:1792]),
                         wait=w3, inc=[(gp, 1)])
                    gp_elo[jt] = (gp, gp.n)
                    P.do("vector", lambda v, Eb=Eb, eh=eh, el=el:
                         v.tensor_sub(el[:, 1792:2048], Eb[:, 1792:2048],
                                      eh[:, 1792:2048]),
                         wait=[], inc=[(ve, 1)])
                    ve_elo[jt] = (ve, ve.n)

                def emit_pv(jt):
                    eh, el = ehi3[jt % 3], elo3[jt % 3]
                    base = jt * 130 + hh * 65
                    w0 = [gp_elo[jt], ve_elo[jt]]
                    if jt == 0:
                        w0 += [xh["hsplit"], vk_done]
                        if hh == 0:
                            w0.append(kn_done)
                    terms = [(vk_hi, eh), (vk_lo, eh), (vk_hi, el)]
                    for ti, (vkk, ee) in enumerate(terms):
                        for ib in range(4):
                            P.do("tensor", lambda t, vkk=vkk, ee=ee, ib=ib,
                                 ti=ti, jt=jt, base=base:
                                 t.matmul(psb[ib][0:65, :],
                                          vkk[:, base:base + 65],
                                          ee[:, ib * 512:(ib + 1) * 512],
                                          start=(jt == 0 and ti == 0),
                                          stop=(jt == 15 and ti == 2)),
                                 wait=(w0 if (ti == 0 and ib == 0) else []),
                                 inc=([(te, 1)] if (ti == 2 and ib == 3) else []))
                    te_pv[jt] = (te, te.n)

                for jt in range(16):
                    emit_scores(jt)
                    emit_softmax(jt)
                    if jt == 1 and pending_tail[0] is not None:
                        xh["hsplit"] = pending_tail[0]()
                        hs_st_tail = pending_tail[0].holder["st"]
                        pending_tail[0] = None
                    if dbg_scratch and s == 0 and pair == 0 and hh == 0 and jt == 0:
                        dma("scalar", dbg_sd, sdec[0][:, :], wait=[dk_b[0][3]])
                        dma("scalar", dbg_eh, ehi3[0][:, :], wait=[ve_ehi[0]])
                        dma("scalar", dbg_el, elo3[0][:, :],
                            wait=[gp_elo[0], ve_elo[0]])
                    if jt - 2 >= 0:
                        emit_pv(jt - 2)
                emit_pv(14)
                emit_pv(15)

                def recip4(v):
                    ins = None
                    for ib in range(4):
                        ins = v.reciprocal(rden[0:1, ib * 512:(ib + 1) * 512],
                                           psb[ib][64:65, :])
                    return ins
                hs_drained = [hs_st_tail] if hs_st_tail else []
                P.do("vector", recip4, wait=[te_pv[15]] + hs_drained,
                     inc=[(ve, 1)])
                recip_done = (ve, ve.n)
                rd_st = dma("gpsimd", rden_d, rden[0:1, :],
                            wait=[recip_done])
                bview, _ = bass.broadcast_tensor_aps(rden_d, rb[:, :])
                rb_ld = dma("gpsimd", rb[:, :], bview, wait=[rd_st], ctr=srd)

                def mk_tail(hd=hd, s=s, rb_ld=rb_ld, hs_prev=hs_st_tail):
                    holder = {}

                    def tail():
                        def houtm(v):
                            ins = None
                            for ib in range(4):
                                ins = v.tensor_mul(
                                    psb[ib][64:128, :], psb[ib][0:64, :],
                                    rb[:, ib * 512:(ib + 1) * 512])
                            return ins
                        P.do("vector", houtm, wait=[rb_ld], inc=[(ve, 1)])

                        def hsplit(v):
                            ins = None
                            for ib in range(4):
                                c0 = ib * 512
                                v.tensor_copy(hssb_hi[:, c0:c0 + 512],
                                              psb[ib][64:128, :])
                                ins = v.tensor_sub(hssb_lo[:, c0:c0 + 512],
                                                   psb[ib][64:128, :],
                                                   hssb_hi[:, c0:c0 + 512])
                            return ins
                        P.do("vector", hsplit,
                             wait=([hs_prev] if hs_prev else []),
                             inc=[(ve, 1)])
                        hspl = (ve, ve.n)
                        dma("gpsimd", hs_hi[s, hd], hssb_hi[:, :], wait=[hspl])
                        holder["st"] = dma("gpsimd", hs_lo[s, hd],
                                           hssb_lo[:, :])
                        return hspl
                    tail.holder = holder
                    return tail
                pending_tail[0] = mk_tail()

                xh = {"dk15": dk_b[15][3], "pv15": te_pv[15],
                      "exp14": se_exp[14], "exp15": se_exp[15],
                      "ehi14": ve_ehi[14], "ehi15": ve_ehi[15],
                      "gp14": gp_elo[14], "gp15": gp_elo[15],
                      "vel14": ve_elo[14], "vel15": ve_elo[15],
                      "hsplit": xh["hsplit"]}
        if pending_tail[0] is not None:
            xh["hsplit"] = pending_tail[0]()
            hs_st_tail = pending_tail[0].holder["st"]
            pending_tail[0] = None
        p2_tail_ve = (ve, ve.n)
        p2_tail_te = (te, te.n)
        p2_tail_se = (se, se.n)
    hs_all = (dg, dg.n)
    if upto in ("p1", "p1q", "p2"):
        P.do("sync", None, wait=[hs_all])
        return _finish(nc, P, blk_holder)

    P.scope = "p3a"
    p2w = [p2_tail_ve, p2_tail_te]
    wo_ld1 = dma("sync", wo_hi[:, :], wo_hi_d, wait=p2w + [aw_store_done])
    wo_ld = dma("sync", wo_lo[:, :], wo_lo_d)
    NT = NSEQ * S // 512

    def ln_block(srcbuf, sqbuf, mbuf, rstdbuf, dstbuf, src_ready_ve,
                 apply_extra=(), psum_free=None):
        apply_tails = []
        rl = (list(src_ready_ve) if isinstance(src_ready_ve, list)
              else [src_ready_ve] * 8)
        src_ready_ve = rl[-1]
        sq_ts = []
        sum_tail = None
        for fc in range(8):
            w_sq = [rl[fc]] + ([sum_tail] if sum_tail else [])
            P.do("scalar", lambda sc, fc=fc: sc.activation(
                sqbuf[:, :], srcbuf[:, fc * 512:(fc + 1) * 512], ACT.Square),
                wait=w_sq, inc=[(se, 1)])
            sq_t = (se, se.n)
            P.do("tensor", lambda t, fc=fc: t.matmul(
                psb[2][0:2, :], ones2[:, :],
                srcbuf[:, fc * 512:(fc + 1) * 512],
                start=(fc == 0), stop=(fc == 7)),
                wait=[rl[fc]] + ([psum_free] if (fc == 0 and psum_free)
                                 else []),
                inc=([(te, 1)] if fc == 7 else []))
            P.do("tensor", lambda t, fc=fc: t.matmul(
                psb[3][0:2, :], ones2[:, :], sqbuf[:, :],
                start=(fc == 0), stop=(fc == 7)),
                wait=[sq_t], inc=[(te, 1)])
            sum_tail = (te, te.n)
        stats_mm = (te, te.n)

        def rows(v):
            v.tensor_scalar_mul(mbuf[0:1, :], psb[2][0:1, :], 1.0 / 1024.0)
            v.tensor_scalar_mul(rstdbuf[0:1, :], psb[3][0:1, :], 1.0 / 1024.0)
            v.tensor_mul(sqbuf[0:1, :], mbuf[0:1, :], mbuf[0:1, :])
            v.tensor_sub(rstdbuf[0:1, :], rstdbuf[0:1, :], sqbuf[0:1, :])
            return v.tensor_scalar_add(rstdbuf[0:1, :], rstdbuf[0:1, :],
                                       epsv[0:1, 0:1])
        P.do("vector", rows, wait=[stats_mm], inc=[(ve, 1)])
        ve_rows = (ve, ve.n)
        s0row = dstbuf[0:1, 512:1024]
        trow = dstbuf[0:1, 0:512]
        P.do("scalar", lambda sc: sc.activation(s0row, rstdbuf[0:1, :],
                                                ACT.Sqrt),
             wait=[ve_rows], inc=[(se, 1)])
        se_sqrt = (se, se.n)

        P.do("vector", lambda v: v.reciprocal(sqbuf[0:1, :], s0row),
             wait=[se_sqrt], inc=[(ve, 1)])
        nrecip_done = (ve, ve.n)

        def newton(v):
            y = sqbuf[0:1, :]
            a = rstdbuf[0:1, :]
            for _ in range(2):
                v.tensor_mul(trow, y, y)
                v.tensor_mul(trow, trow, a)
                v.tensor_scalar(trow, trow, -0.5, 1.5, ALU.mult, ALU.add)
                v.tensor_mul(y, y, trow)
            return v.tensor_copy(rstdbuf[0:1, :], y)
        P.do("vector", newton, wait=[nrecip_done], inc=[(ve, 1)])
        ve_newt = (ve, ve.n)
        P.do("tensor", lambda t: t.matmul(psb[2][:, :], ones1128[:, :],
                                          mbuf[0:1, :], start=True, stop=True),
             wait=[ve_newt], inc=[(te, 1)])
        P.do("tensor", lambda t: t.matmul(psb[3][:, :], ones1128[:, :],
                                          rstdbuf[0:1, :], start=True, stop=True),
             inc=[(te, 1)])
        bc_t = (te, te.n)

        def evac_bc(v):
            v.tensor_copy(mbuf[:, :], psb[2][:, :])
            return v.tensor_copy(rstdbuf[:, :], psb[3][:, :])
        P.do("vector", evac_bc, wait=[bc_t], inc=[(ve, 1)])
        bc_v = (ve, ve.n)
        for fc in range(8):
            def apply(v, fc=fc):
                v.tensor_sub(dstbuf[:, fc * 512:(fc + 1) * 512],
                             srcbuf[:, fc * 512:(fc + 1) * 512], mbuf[:, :])
                return v.tensor_mul(dstbuf[:, fc * 512:(fc + 1) * 512],
                                    dstbuf[:, fc * 512:(fc + 1) * 512],
                                    rstdbuf[:, :])
            P.do("vector", apply,
                 wait=([bc_v] + list(apply_extra)) if fc == 0 else [],
                 inc=[(ve, 1)])
            apply_tails.append((ve, ve.n))
        return (ve, ve.n), (te, te.n), apply_tails

    prev_ln_ve = p2_tail_ve
    prev_womm = None
    womm_tail = {}
    tp3_tail = {}
    y1st_prev = None
    for mt in range(NT):
        s = mt // 4
        tok0 = (mt % 4) * 512
        att_hi = att_hi2[mt % 2]
        att_lo = att_lo2[mt % 2]
        att_w = [hs_all] + p2w + ([womm_tail[mt - 2]] if mt >= 2 else [])
        att_ld = None
        for kc in range(8):
            dma("gpsimd", att_hi[:, kc * 512:(kc + 1) * 512],
                hs_hi[s, 2 * kc:2 * kc + 2, :, tok0:tok0 + 512]
                .rearrange("a q t -> (a q) t"),
                wait=(att_w if kc == 0 else []), ctr=satt[mt % 2])
            att_ld = dma("gpsimd", att_lo[:, kc * 512:(kc + 1) * 512],
                         hs_lo[s, 2 * kc:2 * kc + 2, :, tok0:tok0 + 512]
                         .rearrange("a q t -> (a q) t"), ctr=satt[mt % 2])
        xf3_done = None
        for chv in range(2):
            buf = x3tok[chv]
            lw = p2w + ([tp3_tail[(mt - 1, chv)]] if (mt - 1, chv) in tp3_tail else [])
            xl = dma("sync", buf[:, :].rearrange("p (a d) -> p a d", a=2),
                     x_in[s, tok0 + chv * 256:tok0 + (chv + 1) * 256, :]
                     .rearrange("(a p) d -> p a d", p=128), wait=lw)
            for f in range(8):
                bank = f % 2
                for tt in range(2):
                    if f == 0 and tt == 0:
                        w0 = [xl, prev_ln_ve, wo_ld, (ve, ve.n)]
                    elif tt == 0:
                        w0 = [(ve, ve.n)]
                    else:
                        w0 = []
                    P.do("tensor", lambda t, f=f, tt=tt, bank=bank, b=buf:
                         t.transpose(psa[2 + bank][:, tt * 128:(tt + 1) * 128],
                                     b[:, tt * D + f * 128:tt * D + (f + 1) * 128],
                                     eye32[:, :]),
                         wait=w0, inc=([(te, 1)] if tt == 1 else []))
                tpd = (te, te.n)
                P.do("vector", lambda v, f=f, chv=chv, bank=bank:
                     v.tensor_copy(xf3[:, f * 512 + chv * 256:
                                       f * 512 + (chv + 1) * 256],
                                   psa[2 + bank][:, 0:256]),
                     wait=[tpd], inc=[(ve, 1)])
            tp3_tail[(mt, chv)] = (te, te.n)
            xf3_done = (ve, ve.n)

        r1_tails = []
        for fc in range(8):
            bank = fc % 2
            terms = []
            for kc in range(8):
                terms += [(wo_hi, att_hi, kc), (wo_hi, att_lo, kc),
                          (wo_lo, att_hi, kc)]
            for ti, (lh, rh, kc) in enumerate(terms):
                w0 = []
                if ti == 0:
                    w0 = [att_ld, wo_ld, xf3_done]
                    if fc >= 2:
                        w0.append(r1_tails[fc - 2])
                P.do("tensor", lambda t, lh=lh, rh=rh, kc=kc, fc=fc, bank=bank,
                     first=(ti == 0), last=(ti == len(terms) - 1):
                     t.matmul(psa[bank][:, :],
                              lh[:, kc * D + fc * 128:kc * D + (fc + 1) * 128],
                              rh[:, kc * 512:(kc + 1) * 512],
                              start=first, stop=last),
                     wait=w0, inc=([(te, 1)] if ti == len(terms) - 1 else []))
            wo_mm = (te, te.n)
            P.do("vector", lambda v, fc=fc, bank=bank: v.tensor_add(
                r1b[:, fc * 512:(fc + 1) * 512], psa[bank][:, :],
                xf3[:, fc * 512:(fc + 1) * 512]),
                wait=[wo_mm], inc=[(ve, 1)])
            r1_tails.append((ve, ve.n))
        prev_womm = (te, te.n)
        womm_tail[mt] = prev_womm
        r1_done = r1_tails[-1]

        ap_extra = [y1st_prev] if y1st_prev else []
        ln1_ve, ln1_te, _ = ln_block(r1b, sq3, mb3, rstdb3, y1b, r1_tails,
                                     apply_extra=ap_extra,
                                     psum_free=prev_ln_ve)
        for fc in range(8):
            y1st_prev = dma(
                "sync", y1_d[fc * 128:(fc + 1) * 128, s * S + tok0:s * S + tok0 + 512],
                y1b[:, fc * 512:(fc + 1) * 512],
                wait=([ln1_ve] if fc == 0 else []))
        prev_ln_ve = ln1_ve
    y1_all = (dl, dl.n)
    p3a_te = (te, te.n)
    p3a_ve = (ve, ve.n)
    if upto == "p3a":
        P.do("sync", None, wait=[y1_all])
        return _finish(nc, P, blk_holder)

    P.scope = "p3b"
    prev_gu_tail = None
    prev_ln2_ve = p3a_ve
    prev_ln2_tails = [p3a_ve] * 8
    prev_yst = None
    prev_outtp = None
    prev_dn_done = None
    prev_zmul = None
    for mt in range(NT):
        s = mt // 4
        tok0 = (mt % 4) * 512
        wG = [p3a_te, p3a_ve, ffnw_done] if mt == 0 else [gu_tail]
        if dbg_scratch and mt == 4:
            wG = wG + dbg_mt_w
        wg_ld = None
        for c in range(8):
            dma("sync", wgsb[:, c * HID:(c + 1) * HID],
                wgt_d[c * 128:(c + 1) * 128, :], wait=(wG if c == 0 else []),
                ctr=swg[mt % 2])
            wg_ld = dma("sync", wusb[:, c * HID:(c + 1) * HID],
                        wut_d[c * 128:(c + 1) * 128, :], ctr=swg[mt % 2])
        dbg_w = dbg_mt_w if dbg_scratch and mt == 4 else []
        for fc in range(8):
            y1_ld = dma("scalar", y1sb[:, fc * 512:(fc + 1) * 512],
                        y1_d[fc * 128:(fc + 1) * 128,
                             s * S + tok0:s * S + tok0 + 512],
                        wait=[y1_all, prev_ln2_tails[fc]] +
                             (dbg_w if fc == 0 else []))

        def y1split(v):
            ins = None
            for fc in range(8):
                v.tensor_copy(y1h[:, fc * 512:(fc + 1) * 512],
                              y1sb[:, fc * 512:(fc + 1) * 512])
                ins = v.tensor_sub(y1l[:, fc * 512:(fc + 1) * 512],
                                   y1sb[:, fc * 512:(fc + 1) * 512],
                                   y1h[:, fc * 512:(fc + 1) * 512])
            return ins
        P.do("vector", y1split, wait=[y1_ld], inc=[(ve, 1)])
        y1sp = (ve, ve.n)
        dbg_mt_w = []
        if dbg_scratch and mt == 3:
            dbg_mt_w.append(dma("scalar", dbg_y1in, y1sb[:, :], wait=[y1sp]))
            dbg_mt_w.append(dma("scalar", dbg_y1h, y1h[:, :], wait=[y1sp]))
            dbg_mt_w.append(dma("scalar", dbg_y1l, y1l[:, :], wait=[y1sp]))

        hp_tails = []
        for m in range(24):
            bank = m % 2
            for which in range(2):
                wsb = wgsb if which == 0 else wusb
                po = 0 if which == 0 else 1024
                terms = []
                for c in range(8):
                    terms += [(c, y1h), (c, y1l)]
                for ti, (c, rh) in enumerate(terms):
                    w0 = []
                    if ti == 0 and which == 0:
                        w0 = [y1sp, wg_ld]
                        if m >= 2:
                            w0.append(hp_tails[m - 2])
                        if m == 0 and prev_zmul:
                            w0.append(prev_zmul)
                    P.do("tensor", lambda t, m=m, c=c, rh=rh, wsb=wsb, po=po,
                         bank=bank, first=(ti == 0), last=(ti == len(terms) - 1):
                         t.matmul((psa[bank] if po == 0 else psa[2 + bank])[:, :],
                                  wsb[:, c * HID + m * 128:c * HID + (m + 1) * 128],
                                  rh[:, c * 512:(c + 1) * 512],
                                  start=first, stop=last),
                         wait=w0, inc=([(te, 1)] if (ti == len(terms) - 1 and
                                                     which == 1) else []))
            gu_mm = (te, te.n)
            P.do("scalar", lambda sc, bank=bank: sc.activation(
                silb[:, :], psa[bank][:, :], ACT.Silu,
                scale=sv[:, 0:1]),
                wait=[gu_mm] + ([hp_tails[m - 1]] if m >= 1 else []),
                inc=[(se, 1)])
            sil_t = (se, se.n)

            def hprime(v, m=m, bank=bank):
                v.tensor_mul(hpb[:, :], silb[:, :], psa[2 + bank][:, :])
                v.tensor_copy(hph[:, m * 512:(m + 1) * 512], hpb[:, :])
                return v.tensor_sub(hpl[:, m * 512:(m + 1) * 512], hpb[:, :],
                                    hph[:, m * 512:(m + 1) * 512])
            P.do("vector", hprime, wait=[sil_t], inc=[(ve, 1)])
            hp_tails.append((ve, ve.n))
        gu_tail = (te, te.n)
        prev_gu_tail = gu_tail
        if dbg_scratch and mt == 3:
            dbg_mt_w.append(dma("scalar", dbg_hph, hph[:, :], wait=[hp_tails[23]]))
            dbg_mt_w.append(dma("scalar", dbg_hpl, hpl[:, :], wait=[hp_tails[23]]))

        dn_m_tail = {}
        wd_lds = {}
        wl0 = [prev_dn_done] if prev_dn_done is not None else []
        wd_lds[0] = dma("sync", wdsb[0][:, :], wdt_d[0:128, :], wait=wl0,
                        ctr=swd[0])
        wd_lds[1] = dma("sync", wdsb[1][:, :], wdt_d[128:256, :], wait=wl0,
                        ctr=swd[1])
        for m in range(24):
            wsb = wdsb[m % 2]
            for fc in range(8):
                pbank = psa[fc] if fc < 4 else psb[fc - 4]
                for ri, rh in enumerate((hph, hpl)):
                    w0 = []
                    if fc == 0 and ri == 0:
                        w0 = [hp_tails[m], wd_lds[m]]
                        if m == 0:
                            w0.append(hp_tails[23])
                    P.do("tensor", lambda t, m=m, fc=fc, rh=rh, wsb=wsb,
                         pbank=pbank, ri=ri:
                         t.matmul(pbank[:, :],
                                  wsb[:, fc * 128:(fc + 1) * 128],
                                  rh[:, m * 512:(m + 1) * 512],
                                  start=(m == 0 and ri == 0),
                                  stop=(m == 23 and ri == 1)),
                         wait=w0,
                         inc=([(te, 1)] if (fc == 7 and ri == 1) else []))
            dn_m_tail[m] = (te, te.n)
            if m + 2 < 24:
                wd_lds[m + 2] = dma("sync", wdsb[(m + 2) % 2][:, :],
                                    wdt_d[(m + 2) * 128:(m + 3) * 128, :],
                                    wait=[dn_m_tail[m]], ctr=swd[m % 2])
        dn_done = (te, te.n)
        prev_dn_done = dn_done

        z_tails = []
        for fc in range(8):
            pbank = psa[fc] if fc < 4 else psb[fc - 4]
            db = dbuf[fc % 2]
            w0 = [dn_done]
            if fc >= 2:
                w0.append(z_tails[fc - 2])
            P.do("scalar", lambda sc, db=db, pbank=pbank: sc.mul(
                db[:, :], pbank[:, :], s2v[:, 0:1]),
                wait=w0, inc=[(se, 1)])
            dmul = (se, se.n)
            if fc == 3:
                prev_zmul = dmul
            if fc == 7:
                zmul_last = dmul
            P.do("vector", lambda v, fc=fc, db=db: v.tensor_add(
                y1sb[:, fc * 512:(fc + 1) * 512], db[:, :],
                y1sb[:, fc * 512:(fc + 1) * 512]),
                wait=[dmul], inc=[(ve, 1)])
            z_tails.append((ve, ve.n))
        z_done = z_tails[-1]
        if dbg_scratch and mt == 3:
            dbg_mt_w.append(dma("scalar", dbg_zb, y1sb[:, :], wait=[z_done]))

        ln2_ve, ln2_te, ln2_tails = ln_block(y1sb, zsq, mbf, rstdbf, outf,
                                             z_tails, psum_free=zmul_last)
        prev_ln2_tails = ln2_tails

        ot_ev = None
        for tt in range(4):
            for fg in range(2):
                bank = fg % 2
                for f4 in range(4):
                    fc = fg * 4 + f4
                    if tt == 0 and fg == 0 and f4 == 0:
                        w0 = [ln2_ve]
                    elif f4 == 0:
                        w0 = [(ve, ve.n)]
                    else:
                        w0 = []
                    P.do("tensor", lambda t, tt=tt, fc=fc, f4=f4, bank=bank:
                         t.transpose(psb[2 + bank][:, f4 * 128:(f4 + 1) * 128],
                                     outf[:, fc * 512 + tt * 128:
                                          fc * 512 + (tt + 1) * 128],
                                     eye32[:, :]),
                         wait=w0, inc=([(te, 1)] if f4 == 3 else []))
                tpd = (te, te.n)
                P.do("vector", lambda v, tt=tt, fg=fg, bank=bank:
                     v.tensor_copy(outtok[:, tt * D + fg * 512:
                                          tt * D + (fg + 1) * 512],
                                   psb[2 + bank][:, :]),
                     wait=([tpd, prev_yst] if (tt == 0 and fg == 0 and prev_yst)
                           else [tpd]), inc=[(ve, 1)])
                ot_ev = (ve, ve.n)
        prev_outtp = (te, te.n)
        prev_yst = dma("sync",
                       y_out[s, tok0:tok0 + 512, :].rearrange("(a p) d -> p a d", p=128),
                       outtok[:, :].rearrange("p (a d) -> p a d", a=4),
                       wait=[ot_ev])
        prev_ln2_ve = ln2_ve
        if dbg_scratch and mt == 3:
            dbg_mt_w.append(dma("scalar", dbg_outf, outf[:, :], wait=[ln2_ve]))
            prev_gu_tail = None
        if dbg_scratch and mt == 4:
            pass

    P.do("sync", None, wait=[prev_yst])
    return _finish(nc, P, blk_holder)


_NC_CACHE = None


def kernel(L0_attn_x, L0_attn_W_q, L0_attn_W_k, L0_attn_W_v, L0_attn_W_o,
           L0_attn_W_down_kv, L0_attn_W_up_k, L0_attn_W_up_v,
           L0_ffn_w_gate, L0_ffn_w_up, L0_ffn_w_down):
    global _NC_CACHE
    x = np.ascontiguousarray(np.asarray(L0_attn_x, dtype=np.float32))
    if _NC_CACHE is None:
        _NC_CACHE = build_program()
    nc = _NC_CACHE
    weights = {
        "w_q": np.ascontiguousarray(np.asarray(L0_attn_W_q, np.float32)),
        "w_k": np.ascontiguousarray(np.asarray(L0_attn_W_k, np.float32)),
        "w_v": np.ascontiguousarray(np.asarray(L0_attn_W_v, np.float32)),
        "w_o": np.ascontiguousarray(np.asarray(L0_attn_W_o, np.float32)),
        "w_dn": np.ascontiguousarray(np.asarray(L0_attn_W_down_kv, np.float32)),
        "w_uk": np.ascontiguousarray(np.asarray(L0_attn_W_up_k, np.float32)),
        "w_uv": np.ascontiguousarray(np.asarray(L0_attn_W_up_v, np.float32)),
        "wg_i": np.ascontiguousarray(np.asarray(L0_ffn_w_gate, np.int32)),
        "wu_i": np.ascontiguousarray(np.asarray(L0_ffn_w_up, np.int32)),
        "wd_i": np.ascontiguousarray(np.asarray(L0_ffn_w_down, np.int32)),
    }
    in_maps = [dict(weights, x_in=x[2 * c:2 * c + 2]) for c in range(NCORE)]
    res = run_bass_kernel_spmd(nc, in_maps, core_ids=list(range(NCORE)))
    return np.concatenate([r["y_out"] for r in res.results], axis=0)
